# version 32
# baseline (speedup 1.0000x reference)
import contextlib
import numpy as np
import concourse.bass as bass
import concourse.mybir as mybir
from concourse.bass_utils import run_bass_kernel_spmd

F32 = mybir.dt.float32
BF16 = mybir.dt.bfloat16
AF = mybir.ActivationFunctionType
ALU = mybir.AluOpType
AX = mybir.AxisListType

S, D, NH, HD, NE, EI = 4096, 1024, 16, 64, 32, 512
NCORE = 8
DEBUG_DUMP = ()

_CACHE = {}


class _Ctr:
    def __init__(self, nc, stack, names):
        self.h = {}
        self.v = {}
        self.wm = {}
        for n in names:
            self.h[n] = stack.enter_context(nc.semaphore(n))
            self.v[n] = 0

    def inc(self, inst, name, amt=1):
        inst.then_inc(self.h[name], amt)
        self.v[name] += amt
        return self.v[name]

    def wait(self, eng, name, val):
        if val <= 0:
            return
        key = id(getattr(eng, "engine", None)) if getattr(eng, "engine", None) is not None else id(eng)
        wmk = self.wm.setdefault(key, {})
        if wmk.get(name, 0) >= val:
            return
        eng.wait_ge(self.h[name], val)
        wmk[name] = val

    def snap(self):
        return dict(self.v)


def r3(t, a):
    return t[:].rearrange("p (a n) -> p a n", a=a)


def build_program():
    nc = bass.Bass("TRN2", target_bir_lowering=False, detect_race_conditions=False)

    def din(name, shape, dt=F32):
        return nc.dram_tensor(name, shape, dt, kind="ExternalInput").ap()

    def dscratch(name, shape, dt):
        kind = "ExternalOutput" if name in DEBUG_DUMP else "Internal"
        return nc.dram_tensor(name, shape, dt, kind=kind).ap()

    xT_d = din("xT", [D, S])
    x_d = din("x", [S, D])
    wq_d = din("wq", [D, D])
    wk_d = din("wk", [D, D])
    wv_d = din("wv", [D, D])
    wo_d = din("wo", [D, D])
    bo_d = din("bo", [1, D])
    gwT_d = din("gwT", [D, NE])
    eb_d = din("eb", [1, NE])
    guT_d = din("guT", [NE, D, D])
    dnT_d = din("dnT", [NE, EI, D])
    sgwT_d = din("sgwT", [D, D])
    suwT_d = din("suwT", [D, D])
    sdwT_d = din("sdwT", [D, D])
    ropeA_d = din("ropeA", [S, HD])
    ropeB_d = din("ropeB", [S, HD])
    dmask_d = din("dmask", [4, 128, 512])
    ones_d = din("ones", [128, 128])
    ones1_d = din("ones1", [1, 128])
    ident_d = din("ident", [128, 128])

    out_d = nc.dram_tensor("out", [S, D], F32, kind="ExternalOutput").ap()

    xnT_s = dscratch("xnT", [D, S], BF16)
    QT_s = dscratch("QT", [D, S], BF16)
    KT_s = dscratch("KT", [D, S], BF16)
    Vn_s = dscratch("VT", [NH, S, HD], BF16)
    ctxT_s = dscratch("ctxT", [D, S], BF16)
    if "cnat" in DEBUG_DUMP:
        cnat_d = nc.dram_tensor("cnat", [128, 1024], F32, kind="ExternalOutput").ap()
    if "logits" in DEBUG_DUMP:
        logits_d = nc.dram_tensor("logits", [128, 1024], F32, kind="ExternalOutput").ap()

    with contextlib.ExitStack() as top:
        E = top.enter_context
        ones_f = E(nc.sbuf_tensor("ones_f", [128, 128], F32))
        ones_b = E(nc.sbuf_tensor("ones_b", [128, 128], BF16))
        ones1_f = E(nc.sbuf_tensor("ones1_f", [1, 128], F32))
        ident_b = E(nc.sbuf_tensor("ident_b", [128, 128], BF16))
        c_nat = E(nc.sbuf_tensor("c_nat", [128, 1024], F32))
        bo_bc = E(nc.sbuf_tensor("bo_bc", [128, D], F32))
        gub0 = E(nc.sbuf_tensor("gub0", [128, 8 * 1024], BF16))
        dnb0 = E(nc.sbuf_tensor("dnb0", [128, 4 * 1024], BF16))
        wo_b = E(nc.sbuf_tensor("wo_b", [128, 8 * 1024], BF16))

        C = _Ctr(nc, top, ["LDC", "LDX0", "LDX1", "WQC", "WQB", "WQW", "WQE0", "WQE1",
                   "SDA", "SDAQ0", "SDAQ1", "SDD", "LDH0", "LDH1", "SDB0", "SDB1", "LDN",
                   "LCC0", "LCC1", "LXR0", "LXR1", "LXR2", "OD0", "OD1", "SDV0", "SDV1", "WQX0", "WQX1",
                   "TT", "SS", "VV", "GG"])
        block = E(nc.Block())

        with contextlib.ExitStack() as pa:
            Ea = pa.enter_context
            xTs = [Ea(nc.sbuf_tensor(f"xTs{i}", [128, 1024], F32)) for i in range(2)]
            xTb2 = [Ea(nc.sbuf_tensor(f"xTb{i}", [128, 1024], BF16)) for i in range(2)]
            xsq2 = [Ea(nc.sbuf_tensor(f"xsq{i}", [128, 1024], BF16)) for i in range(2)]
            wq_b = Ea(nc.sbuf_tensor("wq_b", [128, 8192], BF16))
            wk_b = Ea(nc.sbuf_tensor("wk_b", [128, 8192], BF16))
            wv_b = Ea(nc.sbuf_tensor("wv_b", [128, 8192], BF16))
            gwT_sb = Ea(nc.sbuf_tensor("gwT_sb", [128, 8 * NE], F32))
            ropeA_sb = Ea(nc.sbuf_tensor("ropeA_sb", [128, 2048], F32))
            ropeB_sb = Ea(nc.sbuf_tensor("ropeB_sb", [128, 2048], F32))
            eb_sb = Ea(nc.sbuf_tensor("eb_sb", [1, NE], F32))
            eps_sb = Ea(nc.sbuf_tensor("eps_sb", [128, 1], F32))
            bo_sb = Ea(nc.sbuf_tensor("bo_sb", [1, D], F32))
            gcb = Ea(nc.sbuf_tensor("gcb", [128, NE], F32))
            eb_bc = Ea(nc.sbuf_tensor("eb_bc", [128, NE], F32))
            m_f = Ea(nc.sbuf_tensor("m_f", [128, 128], F32))
            msq_f = Ea(nc.sbuf_tensor("msq_f", [128, 128], F32))
            m2_f = Ea(nc.sbuf_tensor("m2_f", [128, 128], F32))
            var_f = Ea(nc.sbuf_tensor("var_f", [128, 128], F32))
            rstd_f = Ea(nc.sbuf_tensor("rstd_f", [128, 128], F32))
            m_b = Ea(nc.sbuf_tensor("m_b", [128, 128], BF16))
            sm_t = Ea(nc.sbuf_tensor("sm_t", [128, 8], F32))
            lg1 = Ea(nc.sbuf_tensor("lg1", [128, NE], F32))
            logits_all = Ea(nc.sbuf_tensor("logits_all", [128, 1024], F32))
            qr_sb = Ea(nc.sbuf_tensor("qr_sb", [128, 1024], BF16))
            kr_sb = Ea(nc.sbuf_tensor("kr_sb", [128, 1024], BF16))
            vr_sb = [Ea(nc.sbuf_tensor(f"vr_sb{i}", [128, 1024], BF16)) for i in range(2)]
            rtmp = Ea(nc.sbuf_tensor("rtmp", [128, 1024], BF16))
            qstage = [Ea(nc.sbuf_tensor(f"qstage{i}", [128, 2048], BF16)) for i in range(2)]
            kstage = [Ea(nc.sbuf_tensor(f"kstage{i}", [128, 2048], BF16)) for i in range(2)]
            xnstage = [Ea(nc.sbuf_tensor(f"xnstage{i}", [128, 2048], BF16)) for i in range(2)]
            scores_sb = Ea(nc.sbuf_tensor("scores_sb", [128, 1024], F32))
            sfc_sb = Ea(nc.sbuf_tensor("sfc_sb", [128, 1024], F32))
            gwork = Ea(nc.sbuf_tensor("gwork", [128, 256], F32))
            gmask = Ea(nc.sbuf_tensor("gmask", [128, 256], F32))
            rmax = Ea(nc.sbuf_tensor("rmax", [128, 32], F32))
            isw = Ea(nc.sbuf_tensor("isw", [128, 1024], F32))
            m6 = Ea(nc.sbuf_tensor("m6", [128, 1024], F32))
            tw = Ea(nc.sbuf_tensor("tw", [128, 1024], F32))
            tsum = Ea(nc.sbuf_tensor("tsum", [128, 32], F32))
            rr = Ea(nc.sbuf_tensor("rr", [128, 32], F32))
            tscr = [Ea(nc.sbuf_tensor(f"tscr{i}", [128, 512], F32)) for i in range(2)]

            ident_f = Ea(nc.sbuf_tensor("ident_f", [128, 128], F32))
            pP = [Ea(nc.psum_tensor(f"pP{i}", [128, 1024], F32)) for i in range(2)]
            pS = [Ea(nc.psum_tensor(f"pS{i}", [128, 512], F32)) for i in range(2)]
            pT = [Ea(nc.psum_tensor(f"pT{i}", [128, 512], BF16)) for i in range(2)]

            @block.sync
            def _(sy):
                C.inc(sy.dma_start(out=ones_f[:], in_=ones_d), "LDC", 16)
                C.inc(sy.dma_start(out=ones1_f[:], in_=ones1_d), "LDC", 16)
                C.inc(sy.dma_start(out=gwT_sb[:].rearrange("p (a n) -> p a n", a=8),
                                   in_=gwT_d.rearrange("(a p) n -> p a n", p=128)), "LDC", 16)
                C.inc(sy.dma_start(out=ropeA_sb[:].rearrange("p (a n) -> p a n", a=32),
                                   in_=ropeA_d.rearrange("(a p) n -> p a n", p=128)), "LDC", 16)
                C.inc(sy.dma_start(out=ropeB_sb[:].rearrange("p (a n) -> p a n", a=32),
                                   in_=ropeB_d.rearrange("(a p) n -> p a n", p=128)), "LDC", 16)
                C.inc(sy.dma_start(out=eb_sb[:], in_=eb_d), "LDC", 16)
                C.inc(sy.dma_start(out=bo_sb[:], in_=bo_d), "LDC", 16)
                C.inc(sy.dma_start(out=ident_f[:], in_=ident_d), "LDC", 16)
            ld_consts = C.v["LDC"]

            @block.gpsimd
            def _(g):
                i = g.dma_start(out=ones_b[:], in_=ones_d)
                C.inc(i, "WQC", 16)
                i = g.dma_start(out=ident_b[:], in_=ident_d)
                C.inc(i, "WQC", 16)
                i = g.dma_start(out=wq_b[:].rearrange("p (a n) -> p a n", a=8),
                                in_=wq_d.rearrange("(a p) n -> p a n", p=128))
                C.inc(i, "WQC", 16)
                i = g.dma_start(out=wk_b[:].rearrange("p (a n) -> p a n", a=8),
                                in_=wk_d.rearrange("(a p) n -> p a n", p=128))
                C.inc(i, "WQC", 16)
                i = g.dma_start(out=wv_b[:].rearrange("p (a n) -> p a n", a=8),
                                in_=wv_d.rearrange("(a p) n -> p a n", p=128))
                C.inc(i, "WQC", 16)
            wq_loads = C.v["WQC"]

            @block.tensor
            def _(t):
                C.wait(t, "LDC", ld_consts)
                gw3 = gwT_sb[:].rearrange("p (a n) -> p a n", a=8)
                for k in range(8):
                    t.matmul(out=pS[0][:, 0:NE], lhsT=ones_f[:], rhs=gw3[:, k, :],
                             start=(k == 0), stop=(k == 7))
                t.matmul(out=pS[0][:, 64:64 + NE], lhsT=ones1_f[:], rhs=eb_sb[:],
                         start=True, stop=True)
                t.matmul(out=pP[0][:, 0:512], lhsT=ones1_f[:], rhs=bo_sb[:, 0:512],
                         start=True, stop=True)
                i = t.matmul(out=pP[0][:, 512:1024], lhsT=ones1_f[:], rhs=bo_sb[:, 512:1024],
                             start=True, stop=True)
                C.inc(i, "TT")
            tt_const = C.v["TT"]

            @block.vector
            def _(v):
                C.wait(v, "TT", tt_const)
                v.memset(eps_sb[:], 1e-5)
                v.tensor_copy(out=gcb[:], in_=pS[0][:, 0:NE])
                v.tensor_copy(out=eb_bc[:], in_=pS[0][:, 64:64 + NE])
                i = v.tensor_copy(out=bo_bc[:], in_=pP[0][:])
                C.inc(i, "VV")
            vv_const = C.v["VV"]

            xt_free_tt = [0, 0]
            stage_sdp = [0, 0]
            pt_free_ss = [0, 0]
            prev_vxn = 0
            prev_vxn2 = [0, 0]
            vdma_sd = [0, 0]
            NJ = 32
            ld_x_arr = [0] * NJ
            ss_sq_arr = [0] * NJ
            tt_stats_arr = [0] * NJ

            def front_dmas(j):
                par = j % 2

                @block.sync
                def _(sy, j=j, par=par, xf=xt_free_tt[par]):
                    C.wait(sy, "TT", xf)
                    i = sy.dma_start(
                        out=xTs[par][:].rearrange("p (a n) -> p a n", a=8),
                        in_=xT_d.rearrange("(a p) n -> p a n", p=128)[:, :, j * 128:(j + 1) * 128])
                    C.inc(i, f"LDX{par}", 16)
                ld_x_arr[j] = C.v[f"LDX{par}"]

                @block.gpsimd
                def _(g, j=j, par=par, xbf=prev_vxn2[par]):
                    if xbf > 0:
                        C.wait(g, "VV", xbf)
                    i = g.dma_start(
                        out=xTb2[par][:].rearrange("p (a n) -> p a n", a=8),
                        in_=xT_d.rearrange("(a p) n -> p a n", p=128)[:, :, j * 128:(j + 1) * 128])
                    C.inc(i, f"WQX{par}", 16)

            def front_sq(j):
                par = j % 2

                @block.scalar
                def _(s, j=j, par=par, ldxb=C.v[f"WQX{j % 2}"],
                      ptt=tt_stats_arr[j - 2] if j >= 2 else 0):
                    C.wait(s, f"WQX{par}", ldxb)
                    if ptt > 0:
                        C.wait(s, "TT", ptt)
                    i = s.activation(out=xsq2[par][:], in_=xTb2[par][:], func=AF.Square)
                    C.inc(i, "SS")
                ss_sq_arr[j] = C.v["SS"]

            def front_stats(j):
                par = j % 2

                @block.tensor
                def _(t, j=j, par=par, pvx=prev_vxn2[par]):
                    C.wait(t, "SS", ss_sq_arr[j])
                    C.wait(t, f"LDX{par}", ld_x_arr[j])
                    if j == 0:
                        C.wait(t, "VV", vv_const)
                        C.wait(t, "WQC", wq_loads)
                    if pvx > 0:
                        C.wait(t, "VV", pvx)
                    xb3 = xTb2[par][:].rearrange("p (a n) -> p a n", a=8)
                    sq3 = xsq2[par][:].rearrange("p (a n) -> p a n", a=8)
                    xs3 = xTs[par][:].rearrange("p (a n) -> p a n", a=8)
                    gw3 = gwT_sb[:].rearrange("p (a n) -> p a n", a=8)
                    for k in range(8):
                        t.matmul(out=pS[par][:, 0:128], lhsT=ones_b[:], rhs=xb3[:, k, :],
                                 start=(k == 0), stop=(k == 7))
                    for k in range(8):
                        t.matmul(out=pS[par][:, 128:256], lhsT=ones_b[:], rhs=sq3[:, k, :],
                                 start=(k == 0), stop=(k == 7))
                    last = None
                    for k in range(8):
                        last = t.matmul(out=pS[par][:, 288:288 + NE], lhsT=xs3[:, k, :],
                                        rhs=gw3[:, k, :], start=(k == 0), stop=(k == 7))
                    C.inc(last, "TT")
                tt_stats_arr[j] = C.v["TT"]
                xt_free_tt[par] = C.v["TT"]

            front_dmas(0)
            front_sq(0)
            front_stats(0)

            for jj in range(NJ):
                par = jj % 2
                cpos = jj % 4
                tt_stats = tt_stats_arr[jj]

                if jj + 1 < NJ:
                    front_dmas(jj + 1)

                @block.vector
                def _(v, par=par):
                    C.wait(v, "TT", tt_stats)
                    v.tensor_scalar_mul(out=m_f[:], in0=pS[par][:, 0:128], scalar1=1.0 / D)
                    v.tensor_scalar_mul(out=msq_f[:], in0=pS[par][:, 128:256], scalar1=1.0 / D)
                    v.tensor_tensor(out=m2_f[:], in0=m_f[:], in1=m_f[:], op=ALU.mult)
                    i = v.tensor_tensor(out=var_f[:], in0=msq_f[:], in1=m2_f[:], op=ALU.subtract)
                    C.inc(i, "VV")
                vv_st1 = C.v["VV"]

                @block.scalar
                def _(s):
                    C.wait(s, "VV", vv_st1)
                    s.activation(out=rstd_f[:], in_=var_f[:], func=AF.Ln, bias=eps_sb[:])
                    i = s.activation(out=rstd_f[:], in_=rstd_f[:], func=AF.Exp, scale=-0.5)
                    C.inc(i, "SS")
                ss_rstd = C.v["SS"]
                if jj + 1 < NJ:
                    front_sq(jj + 1)

                @block.tensor
                def _(t, par=par):
                    C.wait(t, "SS", ss_rstd)
                    t.matmul(out=pS[par][:, 320:321], lhsT=m_f[:], rhs=ident_f[:, 0:1],
                             start=True, stop=True)
                    i = t.matmul(out=pS[par][:, 321:322], lhsT=rstd_f[:], rhs=ident_f[:, 0:1],
                                 start=True, stop=True)
                    C.inc(i, "TT")
                tt_t1b = C.v["TT"]
                if jj + 1 < NJ:
                    front_stats(jj + 1)

                @block.vector
                def _(v, jj=jj, par=par, cpos=cpos,
                      sdq=stage_sdp[(jj // 2) % 2], tt_t1b=tt_t1b):
                    C.wait(v, "TT", tt_t1b)
                    if cpos % 2 == 0 and sdq > 0:
                        C.wait(v, f"SDAQ{(jj // 2) % 2}", sdq)
                    v.tensor_scalar_mul(out=sm_t[:, 0:1], in0=pS[par][:, 320:321], scalar1=1.0)
                    v.tensor_scalar_mul(out=sm_t[:, 5:6], in0=pS[par][:, 321:322], scalar1=-1.0)
                    v.scalar_tensor_tensor(out=lg1[:], in0=gcb[:], scalar=sm_t[:, 0:1],
                                           in1=pS[par][:, 288:288 + NE], op0=ALU.mult, op1=ALU.subtract)
                    v.tensor_scalar_mul(out=logits_all[:, jj * NE:(jj + 1) * NE], in0=lg1[:],
                                        scalar1=sm_t[:, 5:6])
                    v.tensor_copy(out=m_b[:], in_=m_f[:])
                    xb3 = xTb2[par][:].rearrange("p (a n) -> p a n", a=8)
                    xn3 = xnstage[(jj // 2) % 2][:].rearrange("p (a n) -> p a n", a=8)
                    mb_b = m_b[:].unsqueeze(1).broadcast_to([128, 8, 128])
                    rs_b = rstd_f[:].unsqueeze(1).broadcast_to([128, 8, 128])
                    v.tensor_tensor(out=xb3, in0=xb3, in1=mb_b, op=ALU.subtract)
                    last = v.tensor_tensor(out=xn3[:, :, (cpos % 2) * 128:(cpos % 2 + 1) * 128],
                                           in0=xb3, in1=rs_b, op=ALU.mult)
                    C.inc(last, "VV")
                vv_xn = C.v["VV"]
                prev_vxn = vv_xn
                prev_vxn2[jj % 2] = vv_xn

                def proj(t, wsb, cpos, bank, jj=jj):
                    xn3 = xnstage[(jj // 2) % 2][:].rearrange("p (a n) -> p a n", a=8)
                    w3 = wsb[:].rearrange("p (a n) -> p a n", a=8)
                    last = None
                    for k in range(8):
                        lh = xn3[:, k, (cpos % 2) * 128:(cpos % 2 + 1) * 128]
                        t.matmul(out=pP[bank][:, 0:512], lhsT=lh, rhs=w3[:, k, 0:512],
                                 start=(k == 0), stop=(k == 7))
                        last = t.matmul(out=pP[bank][:, 512:1024], lhsT=lh, rhs=w3[:, k, 512:1024],
                                        start=(k == 0), stop=(k == 7))
                    return last

                @block.tensor
                def _(t, cpos=cpos, jj=jj):
                    C.wait(t, "VV", vv_xn)
                    if jj == 0:
                        C.wait(t, "WQC", wq_loads)
                    C.inc(proj(t, wq_b, cpos, 0), "TT")
                    C.inc(proj(t, wk_b, cpos, 1), "TT")
                tt_q = C.v["TT"] - 1
                tt_k = C.v["TT"]

                def rope_emit(v, dst, jj, bank):
                    ra = ropeA_sb[:, jj * 64:(jj + 1) * 64]
                    rb = ropeB_sb[:, jj * 64:(jj + 1) * 64]
                    ra_b = ra.unsqueeze(1).broadcast_to([128, 16, HD])
                    p3 = pP[bank][:].rearrange("p (h d) -> p h d", h=16)
                    d2 = dst[:]
                    v.tensor_tensor(out=d2.rearrange("p (h d) -> p h d", h=16),
                                    in0=p3, in1=ra_b, op=ALU.mult)
                    p4 = pP[bank][:].rearrange("p (h d two) -> p h d two", h=16, two=2)
                    rb4 = rb.rearrange("p (d two) -> p d two", two=2)
                    t4 = rtmp[:].rearrange("p (h d two) -> p h d two", h=16, two=2)
                    v.tensor_tensor(out=t4[:, :, :, 0:1], in0=p4[:, :, :, 1:2],
                                    in1=rb4[:, :, 0:1].unsqueeze(1).broadcast_to([128, 16, 32, 1]),
                                    op=ALU.mult)
                    v.tensor_tensor(out=t4[:, :, :, 1:2], in0=p4[:, :, :, 0:1],
                                    in1=rb4[:, :, 1:2].unsqueeze(1).broadcast_to([128, 16, 32, 1]),
                                    op=ALU.mult)
                    return v.tensor_tensor(out=d2, in0=d2, in1=rtmp[:], op=ALU.add)

                @block.vector
                def _(v, jj=jj):
                    C.wait(v, "TT", tt_q)
                    C.inc(rope_emit(v, qr_sb, jj, 0), "VV")
                vv_rq = C.v["VV"]

                @block.vector
                def _(v, jj=jj):
                    C.wait(v, "TT", tt_k)
                    C.inc(rope_emit(v, kr_sb, jj, 1), "VV")
                vv_rk = C.v["VV"]

                @block.tensor
                def _(t, cpos=cpos):
                    C.wait(t, "VV", vv_rq)
                    C.inc(proj(t, wv_b, cpos, 0), "TT")
                tt_v = C.v["TT"]

                @block.vector
                def _(v, par=par, sdv=vdma_sd[par]):
                    C.wait(v, "TT", tt_v)
                    if sdv > 0:
                        C.wait(v, f"SDV{par}", sdv)
                    i = v.tensor_copy(out=vr_sb[par][:], in_=pP[0][:])
                    C.inc(i, "VV")
                vv_vr = C.v["VV"]

                @block.sync
                def _(sy, jj=jj, par=par):
                    C.wait(sy, "VV", vv_vr)
                    i = sy.dma_start(
                        out=Vn_s.rearrange("h (tc p) d -> tc p h d", p=128)[jj],
                        in_=vr_sb[par][:].rearrange("p (h d) -> p h d", h=16))
                    C.inc(i, f"SDV{par}", 16)
                vdma_sd[par] = C.v[f"SDV{par}"]

                srcs = [qr_sb, kr_sb]
                ss0 = C.v["SS"]
                evac_pred = [ss0 + i + 1 for i in range(4)]
                round_tt = []

                @block.tensor
                def _(t, pt_free_ss=list(pt_free_ss)):
                    C.wait(t, "VV", vv_vr)
                    for r in range(4):
                        if r >= 2:
                            C.wait(t, "SS", evac_pred[r - 2])
                        elif pt_free_ss[r % 2] > 0:
                            C.wait(t, "SS", pt_free_ss[r % 2])
                        src = srcs[r // 2]
                        last = None
                        for q in range(4):
                            dd = (r % 2) * 4 + q
                            last = t.transpose(out=pT[r % 2][:, q * 128:(q + 1) * 128],
                                               in_=src[:, dd * 128:(dd + 1) * 128],
                                               identity=ident_b[:])
                        C.inc(last, "TT")
                        round_tt.append(C.v["TT"])

                @block.scalar
                def _(s, cpos=cpos, jj=jj, sdq=stage_sdp[(jj // 2) % 2]):
                    stages = [qstage[(jj // 2) % 2], kstage[(jj // 2) % 2]]
                    if cpos % 2 == 0 and sdq > 0:
                        C.wait(s, f"SDAQ{(jj // 2) % 2}", sdq)
                    for r in range(4):
                        C.wait(s, "TT", round_tt[r])
                        stg = stages[r // 2]
                        dst = stg[:].rearrange("p (a j t) -> p a j t", a=8, j=2)[:, (r % 2) * 4:(r % 2) * 4 + 4, cpos % 2, :]
                        i = s.copy(out=dst, in_=pT[r % 2][:].rearrange("p (a t) -> p a t", a=4))
                        C.inc(i, "SS")
                assert C.v["SS"] == evac_pred[-1]
                pt_free_ss = [evac_pred[2], evac_pred[3]]

                if cpos % 2 == 1:
                    ss_now = C.v["SS"]
                    vv_now = vv_xn
                    hg = jj // 2

                    @block.sync
                    def _(sy, hg=hg, ss_now=ss_now, vv_now=vv_now):
                        P = hg % 2
                        C.wait(sy, "SS", ss_now)
                        for stg, dst in ((qstage[P], QT_s), (kstage[P], KT_s)):
                            i = sy.dma_start(
                                out=dst.rearrange("(a p) n -> p a n", p=128)[:, :, hg * 256:(hg + 1) * 256],
                                in_=stg[:].rearrange("p (a n) -> p a n", a=8))
                            C.inc(i, f"SDAQ{P}", 16)
                        C.wait(sy, "VV", vv_now)
                        i = sy.dma_start(
                            out=xnT_s.rearrange("(a p) n -> p a n", p=128)[:, :, hg * 256:(hg + 1) * 256],
                            in_=xnstage[P][:].rearrange("p (a n) -> p a n", a=8))
                        C.inc(i, f"SDAQ{P}", 16)
                    stage_sdp[hg % 2] = C.v[f"SDAQ{hg % 2}"]

            @block.scalar
            def _(s, vxn=prev_vxn):
                C.wait(s, "VV", vxn)
                i = s.activation(out=scores_sb[:], in_=logits_all[:], func=AF.Sigmoid)
                C.inc(i, "SS")
            ss_sig = C.v["SS"]

            @block.vector
            def _(v):
                C.wait(v, "SS", ss_sig)

                def tree(out3, in3, B, op):
                    cur, width, lvl = in3, B, 0
                    while width > 2:
                        half = width // 2
                        A = cur.shape[1]
                        dst = tscr[lvl % 2][:].rearrange("p (a w) -> p a w", a=A)[:, :, 0:half]
                        v.tensor_tensor(out=dst, in0=cur[:, :, 0:half],
                                        in1=cur[:, :, half:width], op=op)
                        cur, width, lvl = dst, half, lvl + 1
                    v.tensor_tensor(out=out3, in0=cur[:, :, 0:1], in1=cur[:, :, 1:2], op=op)

                sfc4 = sfc_sb[:].rearrange("p (c g s) -> p c g s", c=32, g=8)
                sc4 = scores_sb[:].rearrange("p (c g s) -> p c g s", c=32, g=8)
                ebb = eb_bc[:].rearrange("p (g s) -> p g s", g=8).unsqueeze(1).broadcast_to([128, 32, 8, 4])
                v.tensor_tensor(out=sfc4, in0=sc4, in1=ebb, op=ALU.add)
                g3 = gwork[:].rearrange("p (c g) -> p c g", c=32)
                sfc3 = sfc_sb[:].rearrange("p (cg s) -> p cg s", s=4)
                tree(g3.rearrange("p c g -> p (c g)").rearrange("p (cg o) -> p cg o", o=1),
                     sfc3, 4, ALU.max)
                v.memset(gmask[:], 0.0)
                for it in range(4):
                    rm3 = rmax[:].rearrange("p (c o) -> p c o", o=1)
                    tree(rm3, g3, 8, ALU.max)
                    rb_ = rmax[:].unsqueeze(2).broadcast_to([128, 32, 8])
                    iw3 = isw[:, 0:256].rearrange("p (c g) -> p c g", c=32)
                    v.tensor_tensor(out=iw3, in0=g3, in1=rb_, op=ALU.is_equal)
                    v.tensor_tensor(out=gmask[:], in0=gmask[:], in1=isw[:, 0:256], op=ALU.add)
                    v.scalar_tensor_tensor(out=gwork[:], in0=isw[:, 0:256], scalar=-1e9,
                                           in1=gwork[:], op0=ALU.mult, op1=ALU.add)
                gm4 = gmask[:].rearrange("p (c g) -> p c g", c=32).unsqueeze(3).broadcast_to([128, 32, 8, 4])
                msk4 = isw[:].rearrange("p (c g s) -> p c g s", c=32, g=8)
                v.tensor_tensor(out=msk4, in0=sfc4, in1=gm4, op=ALU.mult)
                v.memset(m6[:], 0.0)
                w3 = isw[:].rearrange("p (c e) -> p c e", c=32)
                for it in range(6):
                    rm3 = rmax[:].rearrange("p (c o) -> p c o", o=1)
                    tree(rm3, w3, NE, ALU.max)
                    rb_ = rmax[:].unsqueeze(2).broadcast_to([128, 32, NE])
                    tw3 = tw[:].rearrange("p (c e) -> p c e", c=32)
                    v.tensor_tensor(out=tw3, in0=w3, in1=rb_, op=ALU.is_equal)
                    v.tensor_tensor(out=m6[:], in0=m6[:], in1=tw[:], op=ALU.add)
                    v.scalar_tensor_tensor(out=isw[:], in0=tw[:], scalar=-1e9,
                                           in1=isw[:], op0=ALU.mult, op1=ALU.add)
                v.tensor_tensor(out=tw[:], in0=scores_sb[:], in1=m6[:], op=ALU.mult)
                t3 = tsum[:].rearrange("p (c o) -> p c o", o=1)
                tree(t3, tw[:].rearrange("p (c e) -> p c e", c=32), NE, ALU.add)
                v.reciprocal(out=rr[:], in_=tsum[:])
                rrb = rr[:].unsqueeze(2).broadcast_to([128, 32, NE])
                i = v.scalar_tensor_tensor(out=c_nat[:].rearrange("p (c e) -> p c e", c=32),
                                           in0=tw[:].rearrange("p (c e) -> p c e", c=32),
                                           scalar=2.5, in1=rrb, op0=ALU.mult, op1=ALU.mult)
                C.inc(i, "VV")
            vv_cnat = C.v["VV"]

            if ("cnat" in DEBUG_DUMP) or ("logits" in DEBUG_DUMP):
                @block.sync
                def _(sy):
                    C.wait(sy, "VV", vv_cnat)
                    if "cnat" in DEBUG_DUMP:
                        i = sy.dma_start(out=cnat_d, in_=c_nat[:])
                        C.inc(i, "SDD", 16)
                    if "logits" in DEBUG_DUMP:
                        i = sy.dma_start(out=logits_d, in_=logits_all[:])
                        C.inc(i, "SDD", 16)

            fenceA = C.snap()

        with contextlib.ExitStack() as pb:
            Eb = pb.enter_context
            qhp = [Eb(nc.sbuf_tensor(f"qhp{i}", [128, S], BF16)) for i in range(2)]
            khp = [Eb(nc.sbuf_tensor(f"khp{i}", [128, S], BF16)) for i in range(2)]
            vnp = [[Eb(nc.sbuf_tensor(f"vnp{i}{j}", [128, 32 * 65], BF16)) for j in range(2)]
                   for i in range(2)]
            pt_sb = [Eb(nc.sbuf_tensor(f"ptb{i}", [128, 512], BF16)) for i in range(4)]
            dmask_sb = Eb(nc.sbuf_tensor("dmask_sb", [128, 4 * 512], BF16))
            r_sb = [Eb(nc.sbuf_tensor(f"r_sb{i}", [1, 512], F32)) for i in range(2)]
            ctxs_sb = [Eb(nc.sbuf_tensor(f"ctxs_sb{i}", [64, 512], BF16)) for i in range(2)]
            ctx_stage = [Eb(nc.sbuf_tensor(f"ctx_stage{i}", [64, S], BF16)) for i in range(2)]

            psc = [Eb(nc.psum_tensor(f"psc{i}", [128, 512], F32)) for i in range(6)]
            pctx = [Eb(nc.psum_tensor(f"pctx{i}", [65, 512], F32)) for i in range(2)]

            FENCE_NAMES = ("LDC", "LDX0", "LDX1", "WQC", "WQB", "WQW", "WQE0",
                           "WQE1", "SDA", "SDAQ0", "SDAQ1", "LDH0", "LDH1",
                           "SDB0", "SDB1", "LDN",
                           "LCC0", "LCC1", "LXR0", "LXR1", "SDV0", "SDV1", "SDD",
                           "TT", "SS", "VV", "GG")

            @block.gpsimd
            def _(g):
                for nm in FENCE_NAMES:
                    C.wait(g, nm, fenceA[nm])
                i = g.dma_start(out=dmask_sb[:].rearrange("p (j n) -> p j n", j=4),
                                in_=dmask_d.rearrange("j p n -> p j n"))
                C.inc(i, "WQB", 16)
                i = g.dma_start(out=wo_b[:].rearrange("p (a n) -> p a n", a=8),
                                in_=wo_d.rearrange("(a p) n -> p a n", p=128))
                C.inc(i, "WQW", 16)
                i = g.dma_start(out=gub0[:].rearrange("p (a n) -> p a n", a=8),
                                in_=guT_d[0:1, :, :].rearrange("e (a p) n -> p (e a) n", p=128))
                C.inc(i, "WQE0", 16)
                i = g.dma_start(out=dnb0[:].rearrange("p (a n) -> p a n", a=4),
                                in_=dnT_d[0:1, :, :].rearrange("e (a p) n -> p (e a) n", p=128))
                C.inc(i, "WQE0", 16)
                for i2 in range(2):
                    for j2 in range(2):
                        last = g.memset(vnp[i2][j2][:].rearrange("p (kk o) -> p kk o", o=65)[:, :, 64:65], 1.0)
                C.inc(last, "GG")
            wq_mask = C.v["WQB"]
            gg_ones = C.v["GG"]

            pair_t_end = [0, 0]
            stage_sd_b = [0, 0]
            rb_free_vv = [0, 0]
            ctxs_free_vv = [0, 0]
            pctx_free = [dict(ss=0, vv=0), dict(ss=0, vv=0)]
            psc_carry = [0] * 6
            ptb_carry = [0, 0, 0, 0]
            pending = [None]
            head_mult_vv = {}

            def emit_replicates(t, pend):
                for s_ in range(2):
                    C.wait(t, "SS", pend["ss_evac"][s_])
                    C.wait(t, "VV", pend["vv_recip"][s_])
                    if pend["rb_free"][s_] > 0:
                        C.wait(t, "VV", pend["rb_free"][s_])
                    i = t.matmul(out=pctx[s_][0:64, :], lhsT=ones1_f[0:1, 0:64],
                                 rhs=r_sb[s_][:], start=True, stop=True)
                    C.inc(i, "TT")
                pend["tt_rb"] = C.v["TT"]

            def flush_norm():
                pend = pending[0]
                if pend is None:
                    return
                assert pend["tt_rb"] is not None
                p_, qi = pend["p"], pend["qi"]

                @block.vector
                def _(v, pend=pend, p_=p_, qi=qi,
                      sd0=stage_sd_b[0], sd1=stage_sd_b[1]):
                    C.wait(v, "TT", pend["tt_rb"])
                    for s_ in range(2):
                        C.wait(v, "SS", pend["ss_evac"][s_])
                        if qi == 0 and (sd0 if s_ == 0 else sd1) > 0:
                            C.wait(v, f"SDB{s_}", sd0 if s_ == 0 else sd1)
                        i = v.tensor_tensor(out=ctx_stage[s_][:, qi * 512:(qi + 1) * 512],
                                            in0=ctxs_sb[s_][:], in1=pctx[s_][0:64, :], op=ALU.mult)
                        C.inc(i, "VV")
                        rb_free_vv[s_] = C.v["VV"]
                        ctxs_free_vv[s_] = C.v["VV"]
                        head_mult_vv[2 * p_ + s_] = C.v["VV"]
                pending[0] = None

            def emit_stage_dmas(p_):
                for s_ in range(2):
                    h = 2 * p_ + s_

                    @block.sync
                    def _(sy, h=h, s_=s_, vvh=head_mult_vv[h]):
                        C.wait(sy, "VV", vvh)
                        i = sy.dma_start(out=ctxT_s[h * 64:(h + 1) * 64, :],
                                         in_=ctx_stage[s_][:])
                        C.inc(i, f"SDB{s_}", 16)
                    stage_sd_b[s_] = C.v[f"SDB{s_}"]

            for p in range(NH // 2):
                par = p % 2
                hA, hB = 2 * p, 2 * p + 1

                @block.sync
                def _(sy, p=p, par=par, hA=hA, hB=hB):
                    if p == 0:
                        for nm in FENCE_NAMES:
                            C.wait(sy, nm, fenceA[nm])
                    C.wait(sy, "TT", pair_t_end[par])
                    i = sy.dma_start(out=qhp[par][:], in_=QT_s[hA * 64:hA * 64 + 128, :])
                    C.inc(i, f"LDH{par}", 16)
                    i = sy.dma_start(out=khp[par][:], in_=KT_s[hA * 64:hA * 64 + 128, :])
                    C.inc(i, f"LDH{par}", 16)
                    if p == 0:
                        C.wait(sy, "GG", gg_ones)
                    for s_, h in ((0, hA), (1, hB)):
                        i = sy.dma_start(
                            out=vnp[par][s_][:].rearrange("p (kk o) -> p kk o", o=65)[:, :, 0:64],
                            in_=Vn_s[h].rearrange("(kk p) d -> p kk d", p=128))
                        C.inc(i, f"LDH{par}", 16)
                head_ld = C.v[f"LDH{par}"]

                if pending[0] is not None:
                    @block.tensor
                    def _(t, pend=pending[0]):
                        emit_replicates(t, pend)
                    flush_norm()
                if p >= 1:
                    emit_stage_dmas(p - 1)

                for qi in range(8):
                    nk = 4 * (qi + 1)
                    t0 = C.v["TT"]
                    s0 = C.v["SS"]
                    v0 = C.v["VV"]
                    pend = pending[0]
                    v_base = v0 + (2 if pend is not None else 0)

                    order = []
                    for kc in range(nk):
                        order.append(("sA", kc))
                        order.append(("sB", kc))
                        if kc >= 1:
                            order.append(("cA", kc - 1))
                            order.append(("cB", kc - 1))
                    order.append(("cA", nk - 1))
                    order.append(("cB", nk - 1))
                    defer_at = 4 if pend is not None else None
                    pos = {}
                    idx_tt = t0
                    for idx, op in enumerate(order):
                        if defer_at is not None and idx == defer_at:
                            idx_tt += 2
                        idx_tt += 1
                        pos[op] = idx_tt
                    diag = list(range(nk - 4, nk))
                    pt_ready = {}
                    for kc in range(nk):
                        if kc in diag:
                            j = diag.index(kc)
                            pt_ready[("A", kc)] = ("VV", v_base + 2 * j + 1)
                            pt_ready[("B", kc)] = ("VV", v_base + 2 * j + 2)
                        else:
                            pt_ready[("A", kc)] = ("SS", s0 + 2 * kc + 1)
                            pt_ready[("B", kc)] = ("SS", s0 + 2 * kc + 2)

                    @block.tensor
                    def _(t, p=p, par=par, qi=qi, nk=nk, order=order, pos=pos,
                          pt_ready=pt_ready, s0=s0, defer_at=defer_at, pend=pend,
                          pscc=list(psc_carry), pctxf=[dict(x) for x in pctx_free],
                          v0=v0, hld=head_ld):
                        if qi == 0:
                            C.wait(t, f"LDH{par}", hld)
                            if p == 0:
                                C.wait(t, "WQB", wq_mask)
                        for idx, (kind, kc) in enumerate(order):
                            if defer_at is not None and idx == defer_at:
                                emit_replicates(t, pend)
                            s_ = 0 if kind[1] == "A" else 1
                            if kind[0] == "s":
                                slot = 3 * s_ + kc % 3
                                if kc >= 3:
                                    C.wait(t, "SS", s0 + 2 * (kc - 3) + 1 + s_)
                                elif pscc[slot] > 0:
                                    C.wait(t, "SS", pscc[slot])
                                i = t.matmul(out=psc[slot][:],
                                             lhsT=khp[par][s_ * 64:s_ * 64 + 64, kc * 128:(kc + 1) * 128],
                                             rhs=qhp[par][s_ * 64:s_ * 64 + 64, qi * 512:(qi + 1) * 512],
                                             start=True, stop=True)
                                C.inc(i, "TT")
                            else:
                                nm, val = pt_ready[("A" if s_ == 0 else "B", kc)]
                                C.wait(t, nm, val)
                                if kc == 0:
                                    if pctxf[s_]["ss"] > 0:
                                        C.wait(t, "SS", pctxf[s_]["ss"])
                                    if pctxf[s_]["vv"] > 0:
                                        C.wait(t, "VV", pctxf[s_]["vv"])
                                    if pend is not None:
                                        C.wait(t, "VV", v0 + 1 + s_)
                                vn3 = vnp[par][s_][:].rearrange("p (kk o) -> p kk o", o=65)
                                i = t.matmul(out=pctx[s_][:],
                                             lhsT=vn3[:, kc, :],
                                             rhs=pt_sb[2 * s_ + kc % 2][:],
                                             start=(kc == 0), stop=(kc == nk - 1))
                                C.inc(i, "TT")
                    flush_norm()
                    assert C.v["TT"] == pos[("cB", nk - 1)], (C.v["TT"], pos)
                    pair_t_end[par] = C.v["TT"]
                    for kc in range(nk):
                        psc_carry[3 * 0 + kc % 3] = s0 + 2 * kc + 1
                        psc_carry[3 * 1 + kc % 3] = s0 + 2 * kc + 2

                    @block.scalar
                    def _(s, nk=nk, pos=pos, s0=s0, ptbc=list(ptb_carry),
                          ctxsf=list(ctxs_free_vv)):
                        for kc in range(nk):
                            for s_ in range(2):
                                slot = 2 * s_ + kc % 2
                                C.wait(s, "TT", pos[("sA" if s_ == 0 else "sB", kc)])
                                if kc >= 2:
                                    C.wait(s, "TT", pos[("cA" if s_ == 0 else "cB", kc - 2)])
                                elif ptbc[slot] > 0:
                                    C.wait(s, "TT", ptbc[slot])
                                i = s.activation(out=pt_sb[slot][:],
                                                 in_=psc[3 * s_ + kc % 3][:],
                                                 func=AF.Exp, scale=0.125)
                                C.inc(i, "SS")
                        for s_ in range(2):
                            C.wait(s, "TT", pos[("cA" if s_ == 0 else "cB", nk - 1)])
                            if ctxsf[s_] > 0:
                                C.wait(s, "VV", ctxsf[s_])
                            i = s.copy(out=ctxs_sb[s_][:], in_=pctx[s_][0:64, :])
                            C.inc(i, "SS")
                    assert C.v["SS"] == s0 + 2 * nk + 2
                    for kc in range(nk):
                        ptb_carry[2 * 0 + kc % 2] = pos[("cA", kc)]
                        ptb_carry[2 * 1 + kc % 2] = pos[("cB", kc)]
                    pctx_free[0] = dict(ss=s0 + 2 * nk + 1, vv=0)
                    pctx_free[1] = dict(ss=s0 + 2 * nk + 2, vv=0)

                    dm3 = dmask_sb[:].rearrange("p (j n) -> p j n", j=4)

                    @block.vector
                    def _(v, nk=nk, diag=diag, s0=s0, pos=pos):
                        for j, kc in enumerate(diag):
                            for s_ in range(2):
                                C.wait(v, "SS", s0 + 2 * kc + 1 + s_)
                                i = v.tensor_tensor(out=pt_sb[2 * s_ + kc % 2][:],
                                                    in0=pt_sb[2 * s_ + kc % 2][:],
                                                    in1=dm3[:, j, :], op=ALU.mult)
                                C.inc(i, "VV")
                        for s_ in range(2):
                            C.wait(v, "TT", pos[("cA" if s_ == 0 else "cB", nk - 1)])
                            i = v.reciprocal(out=r_sb[s_][:], in_=pctx[s_][64:65, :])
                            C.inc(i, "VV")
                    assert C.v["VV"] == v_base + 10
                    vv_recips = (v_base + 9, v_base + 10)
                    pctx_free[0]["vv"] = v_base + 9
                    pctx_free[1]["vv"] = v_base + 10

                    pending[0] = dict(p=p, qi=qi,
                                      vv_recip=vv_recips,
                                      rb_free=tuple(rb_free_vv),
                                      ss_evac=(s0 + 2 * nk + 1, s0 + 2 * nk + 2),
                                      tt_rb=None)

            @block.tensor
            def _(t, pend=pending[0]):
                emit_replicates(t, pend)
            flush_norm()
            emit_stage_dmas(NH // 2 - 1)

            fenceB = C.snap()
        with contextlib.ExitStack() as pc:
            Ec = pc.enter_context
            acc = Ec(nc.sbuf_tensor("acc", [128, 16 * 1024], F32))
            xnTh = Ec(nc.sbuf_tensor("xnTh", [128, 8 * 2048], BF16))
            gub = [gub0, Ec(nc.sbuf_tensor("gub1", [128, 8 * 1024], BF16))]
            dnb = [dnb0, Ec(nc.sbuf_tensor("dnb1", [128, 4 * 1024], BF16))]
            ctxc = [Ec(nc.sbuf_tensor(f"ctxc{i}", [128, 8 * 128], BF16)) for i in range(2)]
            act_sb = Ec(nc.sbuf_tensor("act_sb", [128, 4 * 512], BF16))
            silu_sb = [Ec(nc.sbuf_tensor(f"silu{i}", [128, 512], BF16)) for i in range(2)]
            xres = [Ec(nc.sbuf_tensor(f"xres{i}", [128, 1024], F32)) for i in range(3)]
            outb = [Ec(nc.sbuf_tensor(f"outb{i}", [128, 1024], F32)) for i in range(2)]

            pg = [Ec(nc.psum_tensor(f"pg{i}", [128, 512], F32)) for i in range(2)]
            pu = [Ec(nc.psum_tensor(f"pu{i}", [128, 512], F32)) for i in range(2)]
            pd = [Ec(nc.psum_tensor(f"pd{i}", [128, 1024], F32)) for i in range(2)]

            @block.gpsimd
            def _(g):
                for nm in ("LDC", "LDX0", "LDX1", "WQC", "WQB", "WQW", "WQE0", "WQE1", "SDA", "SDAQ0", "SDAQ1", "LDH0", "LDH1", "SDB0", "SDB1", "LDN", "LCC0", "LCC1", "LXR0", "LXR1", "SDV0", "SDV1", "WQX0", "WQX1", "SDD", "TT", "SS", "VV", "GG"):
                    C.wait(g, nm, fenceB[nm])
            wq_wo = C.v["WQW"]

            half_t_end = 0
            wbuf_free_tt = [0, 0]
            pd_free_vv = [0, 0]
            pg_free_ss = [0, 0]
            pu_free_vv = [0, 0]
            act_free_tt = 0
            out_done_od = 0
            xres_free_vv = [0, 0, 0]
            outb_free_od = [0, 0]

            wo_last = [0, 0]
            for hf in range(2):
                wo_mm_done = {}
                ctx_ld_v = [0, 0]

                def emit_ctx_load(jj, hf=hf):
                    @block.scalar
                    def _(se, jj=jj, hf=hf, wol=wo_last[jj % 2]):
                        if hf == 0 and jj == 0:
                            for nm in ("LDC", "LDX0", "LDX1", "WQC", "WQB", "WQW", "WQE0", "WQE1", "SDA", "SDAQ0", "SDAQ1", "LDH0", "LDH1", "SDB0", "SDB1", "LDN", "LCC0", "LCC1", "LXR0", "LXR1", "SDV0", "SDV1", "WQX0", "WQX1", "SDD", "TT", "SS", "VV", "GG"):
                                C.wait(se, nm, fenceB[nm])
                        if jj >= 2:
                            C.wait(se, "TT", wo_mm_done[jj - 2])
                        elif wol > 0:
                            C.wait(se, "TT", wol)
                        i = se.dma_start(
                            out=ctxc[jj % 2][:].rearrange("p (a n) -> p a n", a=8),
                            in_=ctxT_s.rearrange("(a p) n -> p a n", p=128)[:, :, (hf * 16 + jj) * 128:(hf * 16 + jj + 1) * 128])
                        C.inc(i, f"LCC{jj % 2}", 16)
                    ctx_ld_v[jj % 2] = C.v[f"LCC{jj % 2}"]

                emit_ctx_load(0)
                emit_ctx_load(1)

                @block.scalar
                def _(se, hf=hf, half_t_end=half_t_end):
                    C.wait(se, "TT", half_t_end)
                    i = se.dma_start(
                        out=xnTh[:].rearrange("p (a n) -> p a n", a=8),
                        in_=xnT_s.rearrange("(a p) n -> p a n", p=128)[:, :, hf * 2048:(hf + 1) * 2048])
                    C.inc(i, "LDN", 16)
                ld_xn = C.v["LDN"]

                for jj in range(16):
                    if jj >= 2:
                        emit_ctx_load(jj)

                    @block.tensor
                    def _(t, jj=jj, pdf=pd_free_vv[jj % 2], ldv=ctx_ld_v[jj % 2]):
                        C.wait(t, f"LCC{jj % 2}", ldv)
                        if jj == 0:
                            C.wait(t, "WQW", wq_wo)
                        if pdf > 0:
                            C.wait(t, "VV", pdf)
                        c3 = ctxc[jj % 2][:].rearrange("p (a n) -> p a n", a=8)
                        w3 = wo_b[:].rearrange("p (a n) -> p a n", a=8)
                        last = None
                        for k in range(8):
                            t.matmul(out=pd[jj % 2][:, 0:512], lhsT=c3[:, k, :],
                                     rhs=w3[:, k, 0:512], start=(k == 0), stop=(k == 7))
                            last = t.matmul(out=pd[jj % 2][:, 512:1024], lhsT=c3[:, k, :],
                                            rhs=w3[:, k, 512:1024], start=(k == 0), stop=(k == 7))
                        C.inc(last, "TT")
                    wo_mm_done[jj] = C.v["TT"]

                    @block.vector
                    def _(v, jj=jj, hf=hf, odg=out_done_od):
                        C.wait(v, "TT", wo_mm_done[jj])
                        if hf == 1 and jj == 0 and odg != 0:
                            C.wait(v, "OD0", odg[0])
                            C.wait(v, "OD1", odg[1])
                        i = v.tensor_tensor(out=acc[:, jj * 1024:(jj + 1) * 1024],
                                            in0=pd[jj % 2][:], in1=bo_bc[:], op=ALU.add)
                        C.inc(i, "VV")
                    pd_free_vv[jj % 2] = C.v["VV"]
                wo_last = [wo_mm_done[14], wo_mm_done[15]]

                for e in range(34):
                    par = e % 2

                    if hf == 0 and e == 0:
                        wq_e = C.v["WQE0"]
                        t0 = C.v["TT"]
                        s0 = C.v["SS"]
                        v0 = C.v["VV"]
                    if not (hf == 0 and e == 0):
                        pass

                    @block.gpsimd
                    def _(g, e=e, par=par, hf=hf, wfree=wbuf_free_tt[par]):
                        if hf == 0 and e == 0:
                            return
                        C.wait(g, "TT", wfree)
                        WQE = f"WQE{par}"
                        if e < NE:
                            i = g.dma_start(out=gub[par][:].rearrange("p (a n) -> p a n", a=8),
                                            in_=guT_d[e:e + 1, :, :].rearrange("e (a p) n -> p (e a) n", p=128))
                            C.inc(i, f"WQE{par}", 16)
                            i = g.dma_start(out=dnb[par][:].rearrange("p (a n) -> p a n", a=4),
                                            in_=dnT_d[e:e + 1, :, :].rearrange("e (a p) n -> p (e a) n", p=128))
                            C.inc(i, f"WQE{par}", 16)
                        else:
                            si = e - NE
                            g3 = gub[par][:].rearrange("p (a n) -> p a n", a=8)
                            i = g.dma_start(out=g3[:, :, 0:512],
                                            in_=sgwT_d[:, si * 512:(si + 1) * 512].rearrange("(a p) n -> p a n", p=128))
                            C.inc(i, f"WQE{par}", 16)
                            i = g.dma_start(out=g3[:, :, 512:1024],
                                            in_=suwT_d[:, si * 512:(si + 1) * 512].rearrange("(a p) n -> p a n", p=128))
                            C.inc(i, f"WQE{par}", 16)
                            i = g.dma_start(out=dnb[par][:].rearrange("p (a n) -> p a n", a=4),
                                            in_=sdwT_d[si * 512:(si + 1) * 512, :].rearrange("(a p) n -> p a n", p=128))
                            C.inc(i, f"WQE{par}", 16)
                    wq_e = C.v[f"WQE{par}"]

                    t0 = C.v["TT"]
                    s0 = C.v["SS"]
                    v0 = C.v["VV"]
                    gpos = lambda blk, ip: t0 + blk * 12 + 2 * ip + 1
                    upos = lambda blk, ip: t0 + blk * 12 + 2 * ip + 2
                    dpos = lambda blk, j2: t0 + blk * 12 + 8 + j2 + 1
                    spos = lambda blk, ip: s0 + blk * 4 + ip + 1
                    mpos = lambda blk, ip: v0 + blk * 8 + ip + 1
                    apos = lambda blk, j2: v0 + blk * 8 + 5 + j2

                    @block.tensor
                    def _(t, e=e, par=par, hf=hf, t0=t0, s0=s0, v0=v0,
                          pgf=list(pg_free_ss), puf=list(pu_free_vv),
                          pdf=list(pd_free_vv), ldxn=ld_xn):
                        C.wait(t, f"WQE{par}", wq_e)
                        if e == 0:
                            C.wait(t, "LDN", ldxn)
                            C.wait(t, "VV", vv_cnat)
                        gu3 = gub[par][:].rearrange("p (a n) -> p a n", a=8)
                        dn3 = dnb[par][:].rearrange("p (a n) -> p a n", a=4)
                        xn3 = xnTh[:].rearrange("p (a n) -> p a n", a=8)
                        a3 = act_sb[:].rearrange("p (a n) -> p a n", a=4)
                        for blk in range(4):
                            for ip in range(4):
                                idx = blk * 4 + ip
                                prev2 = idx - 2
                                if prev2 >= 0:
                                    C.wait(t, "SS", s0 + prev2 + 1)
                                    C.wait(t, "VV", v0 + (prev2 // 4) * 8 + (prev2 % 4) + 1)
                                else:
                                    if pgf[idx % 2] > 0:
                                        C.wait(t, "SS", pgf[idx % 2])
                                    if puf[idx % 2] > 0:
                                        C.wait(t, "VV", puf[idx % 2])
                                last = None
                                for k in range(8):
                                    last = t.matmul(out=pg[idx % 2][:], lhsT=gu3[:, k, ip * 128:(ip + 1) * 128],
                                                    rhs=xn3[:, k, blk * 512:(blk + 1) * 512],
                                                    start=(k == 0), stop=(k == 7))
                                C.inc(last, "TT")
                                last = None
                                for k in range(8):
                                    last = t.matmul(out=pu[idx % 2][:], lhsT=gu3[:, k, 512 + ip * 128:512 + (ip + 1) * 128],
                                                    rhs=xn3[:, k, blk * 512:(blk + 1) * 512],
                                                    start=(k == 0), stop=(k == 7))
                                C.inc(last, "TT")
                            for j2 in range(4):
                                C.wait(t, "VV", v0 + blk * 8 + 4)
                                dpar = j2 % 2
                                if j2 >= 2:
                                    C.wait(t, "VV", v0 + blk * 8 + 5 + (j2 - 2))
                                elif blk >= 1:
                                    C.wait(t, "VV", v0 + (blk - 1) * 8 + 5 + (j2 + 2))
                                elif pdf[dpar] > 0:
                                    C.wait(t, "VV", pdf[dpar])
                                last = None
                                for ip in range(4):
                                    lh = a3[:, ip, j2 * 128:(j2 + 1) * 128]
                                    t.matmul(out=pd[dpar][:, 0:512], lhsT=lh,
                                             rhs=dn3[:, ip, 0:512],
                                             start=(ip == 0), stop=(ip == 3))
                                    last = t.matmul(out=pd[dpar][:, 512:1024], lhsT=lh,
                                                    rhs=dn3[:, ip, 512:1024],
                                                    start=(ip == 0), stop=(ip == 3))
                                C.inc(last, "TT")
                    assert C.v["TT"] == t0 + 48, (C.v["TT"], t0)
                    wbuf_free_tt[par] = C.v["TT"]

                    @block.scalar
                    def _(s, t0=t0, v0=v0):
                        for blk in range(4):
                            for ip in range(4):
                                idx = blk * 4 + ip
                                C.wait(s, "TT", gpos(blk, ip))
                                pm = idx - 2
                                if pm >= 0:
                                    C.wait(s, "VV", v0 + (pm // 4) * 8 + (pm % 4) + 1)
                                i = s.activation(out=silu_sb[idx % 2][:], in_=pg[idx % 2][:],
                                                 func=AF.Silu)
                                C.inc(i, "SS")
                    assert C.v["SS"] == s0 + 16
                    pg_free_ss = [0, 0]
                    for idx in range(16):
                        pg_free_ss[idx % 2] = s0 + idx + 1

                    @block.vector
                    def _(v, e=e, hf=hf, t0=t0, s0=s0, aft=act_free_tt):
                        a3 = act_sb[:].rearrange("p (a n) -> p a n", a=4)
                        for blk in range(4):
                            for ip in range(4):
                                C.wait(v, "SS", spos(blk, ip))
                                C.wait(v, "TT", upos(blk, ip))
                                if blk >= 1:
                                    C.wait(v, "TT", t0 + (blk - 1) * 12 + 12)
                                elif aft > 0:
                                    C.wait(v, "TT", aft)
                                idx = blk * 4 + ip
                                i = v.tensor_tensor(out=a3[:, ip, :], in0=silu_sb[idx % 2][:],
                                                    in1=pu[idx % 2][:], op=ALU.mult)
                                C.inc(i, "VV")
                            for j2 in range(4):
                                C.wait(v, "TT", dpos(blk, j2))
                                jl = blk * 4 + j2
                                cc = hf * 16 + jl
                                accs = acc[:, jl * 1024:(jl + 1) * 1024]
                                if e < NE:
                                    i = v.scalar_tensor_tensor(
                                        out=accs, in0=pd[j2 % 2][:],
                                        scalar=c_nat[:, cc * NE + e:cc * NE + e + 1],
                                        in1=accs, op0=ALU.mult, op1=ALU.add)
                                else:
                                    i = v.tensor_tensor(out=accs, in0=pd[j2 % 2][:],
                                                        in1=accs, op=ALU.add)
                                C.inc(i, "VV")
                    assert C.v["VV"] == v0 + 32
                    pu_free_vv = [0, 0]
                    for idx in range(16):
                        pu_free_vv[idx % 2] = v0 + (idx // 4) * 8 + (idx % 4) + 1
                    act_free_tt = wbuf_free_tt[par]
                    pd_free_vv = [apos(3, 2), apos(3, 3)]

                half_t_end = C.v["TT"]
                vv_acc_done = C.v["VV"]

                for jj in range(16):
                    @block.sync
                    def _(sy, jj=jj, hf=hf, xf=xres_free_vv[jj % 3]):
                        if hf == 0 and jj == 0:
                            for nm in ("SDA", "SDAQ0", "SDAQ1", "LDH0", "LDH1", "SDB0", "SDB1", "SDV0", "SDV1", "SDD", "TT", "SS", "VV", "GG"):
                                C.wait(sy, nm, fenceB[nm])
                        if xf > 0:
                            C.wait(sy, "VV", xf)
                        i = sy.dma_start(out=xres[jj % 3][:],
                                         in_=x_d[(hf * 16 + jj) * 128:(hf * 16 + jj + 1) * 128, :])
                        C.inc(i, f"LXR{jj % 3}", 16)
                    ld_xr = C.v[f"LXR{jj % 3}"]

                    @block.vector
                    def _(v, jj=jj, ld_xr=ld_xr, obf=outb_free_od[jj % 2]):
                        C.wait(v, f"LXR{jj % 3}", ld_xr)
                        if jj == 0:
                            C.wait(v, "VV", vv_acc_done)
                        if obf > 0:
                            C.wait(v, f"OD{jj % 2}", obf)
                        i = v.tensor_tensor(out=outb[jj % 2][:],
                                            in0=acc[:, jj * 1024:(jj + 1) * 1024],
                                            in1=xres[jj % 3][:], op=ALU.add)
                        C.inc(i, "VV")
                    xres_free_vv[jj % 3] = C.v["VV"]
                    vv_out = C.v["VV"]

                    @block.gpsimd
                    def _(g, jj=jj, hf=hf, vv_out=vv_out):
                        C.wait(g, "VV", vv_out)
                        i = g.dma_start(out=out_d[(hf * 16 + jj) * 128:(hf * 16 + jj + 1) * 128, :],
                                        in_=outb[jj % 2][:])
                        C.inc(i, f"OD{jj % 2}", 16)
                    outb_free_od[jj % 2] = C.v[f"OD{jj % 2}"]
                out_done_od = (C.v["OD0"], C.v["OD1"])

            od0, od1 = C.v["OD0"], C.v["OD1"]
            sdd = C.v["SDD"]

            @block.sync
            def _(sy):
                C.wait(sy, "OD0", od0)
                C.wait(sy, "OD1", od1)
                if sdd > 0:
                    C.wait(sy, "SDD", sdd)

    return nc


def _host_constants():
    half = 32
    i32 = np.float32(32.0)
    freqs = np.power(np.float32(10000.0), -(np.arange(half, dtype=np.float32) / i32))
    ang = np.arange(S, dtype=np.float32)[:, None] * freqs[None, :]
    cos = np.cos(ang).astype(np.float32)
    sin = np.sin(ang).astype(np.float32)
    ropeA = np.empty((S, HD), np.float32)
    ropeB = np.empty((S, HD), np.float32)
    ropeA[:, 0::2] = cos
    ropeA[:, 1::2] = cos
    ropeB[:, 0::2] = -sin
    ropeB[:, 1::2] = sin
    dmask = np.zeros((4, 128, 512), np.float32)
    for j in range(4):
        kk = np.arange(128)[:, None] + j * 128
        qq = np.arange(512)[None, :]
        dmask[j] = (kk <= qq).astype(np.float32)
    ones = np.ones((128, 128), np.float32)
    ones1 = np.ones((1, 128), np.float32)
    ident = np.eye(128, dtype=np.float32)
    return ropeA, ropeB, dmask, ones, ones1, ident


def kernel(L0_attn_x, L0_attn_Wq, L0_attn_Wk, L0_attn_Wv, L0_attn_Wo, L0_attn_bo,
           L0_ffn_gate_weight, L0_ffn_e_score_correction_bias, L0_ffn_experts_gate_up,
           L0_ffn_experts_down, L0_ffn_shared_gate_w, L0_ffn_shared_up_w,
           L0_ffn_shared_down_w):
    x = np.ascontiguousarray(np.asarray(L0_attn_x, dtype=np.float32))
    wq = np.ascontiguousarray(np.asarray(L0_attn_Wq, dtype=np.float32))
    wk = np.ascontiguousarray(np.asarray(L0_attn_Wk, dtype=np.float32))
    wv = np.ascontiguousarray(np.asarray(L0_attn_Wv, dtype=np.float32))
    wo = np.ascontiguousarray(np.asarray(L0_attn_Wo, dtype=np.float32))
    bo = np.asarray(L0_attn_bo, dtype=np.float32).reshape(1, D)
    gwT = np.ascontiguousarray(np.asarray(L0_ffn_gate_weight, dtype=np.float32).T)
    eb = np.asarray(L0_ffn_e_score_correction_bias, dtype=np.float32).reshape(1, NE)
    guT = np.ascontiguousarray(np.transpose(np.asarray(L0_ffn_experts_gate_up, dtype=np.float32), (0, 2, 1)))
    dnT = np.ascontiguousarray(np.transpose(np.asarray(L0_ffn_experts_down, dtype=np.float32), (0, 2, 1)))
    sgwT = np.ascontiguousarray(np.asarray(L0_ffn_shared_gate_w, dtype=np.float32).T)
    suwT = np.ascontiguousarray(np.asarray(L0_ffn_shared_up_w, dtype=np.float32).T)
    sdwT = np.ascontiguousarray(np.asarray(L0_ffn_shared_down_w, dtype=np.float32).T)
    ropeA, ropeB, dmask, ones, ones1, ident = _host_constants()

    if "nc" not in _CACHE:
        _CACHE["nc"] = build_program()
    nc = _CACHE["nc"]

    shared = dict(wq=wq, wk=wk, wv=wv, wo=wo, bo=bo, gwT=gwT, eb=eb, guT=guT,
                  dnT=dnT, sgwT=sgwT, suwT=suwT, sdwT=sdwT, ropeA=ropeA,
                  ropeB=ropeB, dmask=dmask, ones=ones, ones1=ones1, ident=ident)
    in_maps = []
    for b in range(NCORE):
        m = dict(shared)
        m["x"] = np.ascontiguousarray(x[b])
        m["xT"] = np.ascontiguousarray(x[b].T)
        in_maps.append(m)

    res = run_bass_kernel_spmd(nc, in_maps, core_ids=list(range(NCORE)))
    out = np.stack([np.asarray(res.results[b]["out"]) for b in range(NCORE)], axis=0)
    return out


# version 33 (final)
# speedup vs baseline: 1.1833x; 1.1833x over previous
import contextlib
import numpy as np
import concourse.bass as bass
import concourse.mybir as mybir
from concourse.bass_utils import run_bass_kernel_spmd

F32 = mybir.dt.float32
BF16 = mybir.dt.bfloat16
AF = mybir.ActivationFunctionType
ALU = mybir.AluOpType
AX = mybir.AxisListType

S, D, NH, HD, NE, EI = 4096, 1024, 16, 64, 32, 512
NCORE = 8
DEBUG_DUMP = ()

_CACHE = {}


class _Ctr:
    def __init__(self, nc, stack, names):
        self.h = {}
        self.v = {}
        self.wm = {}
        for n in names:
            self.h[n] = stack.enter_context(nc.semaphore(n))
            self.v[n] = 0

    def inc(self, inst, name, amt=1):
        inst.then_inc(self.h[name], amt)
        self.v[name] += amt
        return self.v[name]

    def wait(self, eng, name, val):
        if val <= 0:
            return
        key = id(getattr(eng, "engine", None)) if getattr(eng, "engine", None) is not None else id(eng)
        wmk = self.wm.setdefault(key, {})
        if wmk.get(name, 0) >= val:
            return
        eng.wait_ge(self.h[name], val)
        wmk[name] = val

    def snap(self):
        return dict(self.v)


def r3(t, a):
    return t[:].rearrange("p (a n) -> p a n", a=a)


def build_program():
    nc = bass.Bass("TRN2", target_bir_lowering=False, detect_race_conditions=False)

    def din(name, shape, dt=F32):
        return nc.dram_tensor(name, shape, dt, kind="ExternalInput").ap()

    def dscratch(name, shape, dt):
        kind = "ExternalOutput" if name in DEBUG_DUMP else "Internal"
        return nc.dram_tensor(name, shape, dt, kind=kind).ap()

    xT_d = din("xT", [D, S])
    x_d = din("x", [S, D])
    wq_d = din("wq", [D, D])
    wk_d = din("wk", [D, D])
    wv_d = din("wv", [D, D])
    wo_d = din("wo", [D, D])
    bo_d = din("bo", [1, D])
    gwT_d = din("gwT", [D, NE])
    eb_d = din("eb", [1, NE])
    guT_d = din("guT", [NE, D, D])
    dnT_d = din("dnT", [NE, EI, D])
    sgwT_d = din("sgwT", [D, D])
    suwT_d = din("suwT", [D, D])
    sdwT_d = din("sdwT", [D, D])
    ropeA_d = din("ropeA", [S, HD])
    ropeB_d = din("ropeB", [S, HD])
    dmask_d = din("dmask", [4, 128, 512])
    ones_d = din("ones", [128, 128])
    ones1_d = din("ones1", [1, 128])
    ident_d = din("ident", [128, 128])

    out_d = nc.dram_tensor("out", [S, D], F32, kind="ExternalOutput").ap()

    xnT_s = dscratch("xnT", [D, S], BF16)
    QT_s = dscratch("QT", [D, S], BF16)
    KT_s = dscratch("KT", [D, S], BF16)
    Vn_s = dscratch("VT", [NH, S, HD], BF16)
    ctxT_s = dscratch("ctxT", [D, S], BF16)
    if "cnat" in DEBUG_DUMP:
        cnat_d = nc.dram_tensor("cnat", [128, 1024], F32, kind="ExternalOutput").ap()
    if "logits" in DEBUG_DUMP:
        logits_d = nc.dram_tensor("logits", [128, 1024], F32, kind="ExternalOutput").ap()

    with contextlib.ExitStack() as top:
        E = top.enter_context
        ones_f = E(nc.sbuf_tensor("ones_f", [128, 128], F32))
        ones_b = E(nc.sbuf_tensor("ones_b", [128, 128], BF16))
        ones1_f = E(nc.sbuf_tensor("ones1_f", [1, 128], F32))
        ident_b = E(nc.sbuf_tensor("ident_b", [128, 128], BF16))
        c_nat = E(nc.sbuf_tensor("c_nat", [128, 1024], F32))
        bo_bc = E(nc.sbuf_tensor("bo_bc", [128, D], F32))
        gub0 = E(nc.sbuf_tensor("gub0", [128, 8 * 1024], BF16))
        dnb0 = E(nc.sbuf_tensor("dnb0", [128, 4 * 1024], BF16))
        wo_b = E(nc.sbuf_tensor("wo_b", [128, 8 * 1024], BF16))

        C = _Ctr(nc, top, ["LDC", "LDX0", "LDX1", "WQC", "WQB", "WQW", "WQE0", "WQE1",
                   "SDA", "SDAQ0", "SDAQ1", "SDD", "LDH0", "LDH1", "SDB0", "SDB1", "LDN",
                   "LCC0", "LCC1", "LXR0", "LXR1", "OD0", "OD1", "SDV0", "SDV1", "WQX0", "WQX1",
                   "TT", "SS", "VV", "GG"])
        block = E(nc.Block())

        with contextlib.ExitStack() as pa:
            Ea = pa.enter_context
            xTs = [Ea(nc.sbuf_tensor(f"xTs{i}", [128, 1024], F32)) for i in range(2)]
            xTb2 = [Ea(nc.sbuf_tensor(f"xTb{i}", [128, 1024], BF16)) for i in range(2)]
            xsq2 = [Ea(nc.sbuf_tensor(f"xsq{i}", [128, 1024], BF16)) for i in range(2)]
            wq_b = Ea(nc.sbuf_tensor("wq_b", [128, 8192], BF16))
            wk_b = Ea(nc.sbuf_tensor("wk_b", [128, 8192], BF16))
            wv_b = Ea(nc.sbuf_tensor("wv_b", [128, 8192], BF16))
            gwT_sb = Ea(nc.sbuf_tensor("gwT_sb", [128, 8 * NE], F32))
            ropeA_sb = Ea(nc.sbuf_tensor("ropeA_sb", [128, 2048], F32))
            ropeB_sb = Ea(nc.sbuf_tensor("ropeB_sb", [128, 2048], F32))
            eb_sb = Ea(nc.sbuf_tensor("eb_sb", [1, NE], F32))
            eps_sb = Ea(nc.sbuf_tensor("eps_sb", [128, 1], F32))
            bo_sb = Ea(nc.sbuf_tensor("bo_sb", [1, D], F32))
            gcb = Ea(nc.sbuf_tensor("gcb", [128, NE], F32))
            eb_bc = Ea(nc.sbuf_tensor("eb_bc", [128, NE], F32))
            m_f = Ea(nc.sbuf_tensor("m_f", [128, 128], F32))
            msq_f = Ea(nc.sbuf_tensor("msq_f", [128, 128], F32))
            m2_f = Ea(nc.sbuf_tensor("m2_f", [128, 128], F32))
            var_f = Ea(nc.sbuf_tensor("var_f", [128, 128], F32))
            rstd_f = Ea(nc.sbuf_tensor("rstd_f", [128, 128], F32))
            m_b = Ea(nc.sbuf_tensor("m_b", [128, 128], BF16))
            sm_t = Ea(nc.sbuf_tensor("sm_t", [128, 8], F32))
            lg1 = Ea(nc.sbuf_tensor("lg1", [128, NE], F32))
            logits_all = Ea(nc.sbuf_tensor("logits_all", [128, 1024], F32))
            qr_sb = Ea(nc.sbuf_tensor("qr_sb", [128, 1024], BF16))
            kr_sb = Ea(nc.sbuf_tensor("kr_sb", [128, 1024], BF16))
            vr_sb = [Ea(nc.sbuf_tensor(f"vr_sb{i}", [128, 1024], BF16)) for i in range(2)]
            rtmp = Ea(nc.sbuf_tensor("rtmp", [128, 1024], BF16))
            qstage = [Ea(nc.sbuf_tensor(f"qstage{i}", [128, 2048], BF16)) for i in range(2)]
            kstage = [Ea(nc.sbuf_tensor(f"kstage{i}", [128, 2048], BF16)) for i in range(2)]
            xnstage = [Ea(nc.sbuf_tensor(f"xnstage{i}", [128, 2048], BF16)) for i in range(2)]
            scores_sb = Ea(nc.sbuf_tensor("scores_sb", [128, 1024], F32))
            sfc_sb = Ea(nc.sbuf_tensor("sfc_sb", [128, 1024], F32))
            gwork = Ea(nc.sbuf_tensor("gwork", [128, 256], F32))
            gmask = Ea(nc.sbuf_tensor("gmask", [128, 256], F32))
            rmax = Ea(nc.sbuf_tensor("rmax", [128, 32], F32))
            isw = Ea(nc.sbuf_tensor("isw", [128, 1024], F32))
            m6 = Ea(nc.sbuf_tensor("m6", [128, 1024], F32))
            tw = Ea(nc.sbuf_tensor("tw", [128, 1024], F32))
            tsum = Ea(nc.sbuf_tensor("tsum", [128, 32], F32))
            rr = Ea(nc.sbuf_tensor("rr", [128, 32], F32))
            tscr = [Ea(nc.sbuf_tensor(f"tscr{i}", [128, 512], F32)) for i in range(2)]

            ident_f = Ea(nc.sbuf_tensor("ident_f", [128, 128], F32))
            pP = [Ea(nc.psum_tensor(f"pP{i}", [128, 1024], F32)) for i in range(2)]
            pS = [Ea(nc.psum_tensor(f"pS{i}", [128, 512], F32)) for i in range(2)]
            pT = [Ea(nc.psum_tensor(f"pT{i}", [128, 512], BF16)) for i in range(2)]

            @block.sync
            def _(sy):
                C.inc(sy.dma_start(out=ones_f[:], in_=ones_d), "LDC", 16)
                C.inc(sy.dma_start(out=ones1_f[:], in_=ones1_d), "LDC", 16)
                C.inc(sy.dma_start(out=gwT_sb[:].rearrange("p (a n) -> p a n", a=8),
                                   in_=gwT_d.rearrange("(a p) n -> p a n", p=128)), "LDC", 16)
                C.inc(sy.dma_start(out=ropeA_sb[:].rearrange("p (a n) -> p a n", a=32),
                                   in_=ropeA_d.rearrange("(a p) n -> p a n", p=128)), "LDC", 16)
                C.inc(sy.dma_start(out=ropeB_sb[:].rearrange("p (a n) -> p a n", a=32),
                                   in_=ropeB_d.rearrange("(a p) n -> p a n", p=128)), "LDC", 16)
                C.inc(sy.dma_start(out=eb_sb[:], in_=eb_d), "LDC", 16)
                C.inc(sy.dma_start(out=bo_sb[:], in_=bo_d), "LDC", 16)
                C.inc(sy.dma_start(out=ident_f[:], in_=ident_d), "LDC", 16)
            ld_consts = C.v["LDC"]

            @block.gpsimd
            def _(g):
                i = g.dma_start(out=ones_b[:], in_=ones_d)
                C.inc(i, "WQC", 16)
                i = g.dma_start(out=ident_b[:], in_=ident_d)
                C.inc(i, "WQC", 16)
                i = g.dma_start(out=wq_b[:].rearrange("p (a n) -> p a n", a=8),
                                in_=wq_d.rearrange("(a p) n -> p a n", p=128))
                C.inc(i, "WQC", 16)
                i = g.dma_start(out=wk_b[:].rearrange("p (a n) -> p a n", a=8),
                                in_=wk_d.rearrange("(a p) n -> p a n", p=128))
                C.inc(i, "WQC", 16)
                i = g.dma_start(out=wv_b[:].rearrange("p (a n) -> p a n", a=8),
                                in_=wv_d.rearrange("(a p) n -> p a n", p=128))
                C.inc(i, "WQC", 16)
            wq_loads = C.v["WQC"]

            @block.tensor
            def _(t):
                C.wait(t, "LDC", ld_consts)
                gw3 = gwT_sb[:].rearrange("p (a n) -> p a n", a=8)
                for k in range(8):
                    t.matmul(out=pS[0][:, 0:NE], lhsT=ones_f[:], rhs=gw3[:, k, :],
                             start=(k == 0), stop=(k == 7))
                t.matmul(out=pS[0][:, 64:64 + NE], lhsT=ones1_f[:], rhs=eb_sb[:],
                         start=True, stop=True)
                t.matmul(out=pP[0][:, 0:512], lhsT=ones1_f[:], rhs=bo_sb[:, 0:512],
                         start=True, stop=True)
                i = t.matmul(out=pP[0][:, 512:1024], lhsT=ones1_f[:], rhs=bo_sb[:, 512:1024],
                             start=True, stop=True)
                C.inc(i, "TT")
            tt_const = C.v["TT"]

            @block.vector
            def _(v):
                C.wait(v, "TT", tt_const)
                v.memset(eps_sb[:], 1e-5)
                v.tensor_copy(out=gcb[:], in_=pS[0][:, 0:NE])
                v.tensor_copy(out=eb_bc[:], in_=pS[0][:, 64:64 + NE])
                i = v.tensor_copy(out=bo_bc[:], in_=pP[0][:])
                C.inc(i, "VV")
            vv_const = C.v["VV"]

            xt_free_tt = [0, 0]
            stage_sdp = [0, 0]
            pt_free_ss = [0, 0]
            prev_vxn = 0
            prev_vxn2 = [0, 0]
            vdma_sd = [0, 0]
            NJ = 32
            ld_x_arr = [0] * NJ
            ss_sq_arr = [0] * NJ
            tt_stats_arr = [0] * NJ

            def front_dmas(j):
                par = j % 2

                @block.sync
                def _(sy, j=j, par=par, xf=xt_free_tt[par]):
                    C.wait(sy, "TT", xf)
                    i = sy.dma_start(
                        out=xTs[par][:].rearrange("p (a n) -> p a n", a=8),
                        in_=xT_d.rearrange("(a p) n -> p a n", p=128)[:, :, j * 128:(j + 1) * 128])
                    C.inc(i, f"LDX{par}", 16)
                ld_x_arr[j] = C.v[f"LDX{par}"]

                @block.gpsimd
                def _(g, j=j, par=par, xbf=prev_vxn2[par]):
                    if xbf > 0:
                        C.wait(g, "VV", xbf)
                    i = g.dma_start(
                        out=xTb2[par][:].rearrange("p (a n) -> p a n", a=8),
                        in_=xT_d.rearrange("(a p) n -> p a n", p=128)[:, :, j * 128:(j + 1) * 128])
                    C.inc(i, f"WQX{par}", 16)

            def front_sq(j):
                par = j % 2

                @block.scalar
                def _(s, j=j, par=par, ldxb=C.v[f"WQX{j % 2}"],
                      ptt=tt_stats_arr[j - 2] if j >= 2 else 0):
                    C.wait(s, f"WQX{par}", ldxb)
                    if ptt > 0:
                        C.wait(s, "TT", ptt)
                    i = s.activation(out=xsq2[par][:], in_=xTb2[par][:], func=AF.Square)
                    C.inc(i, "SS")
                ss_sq_arr[j] = C.v["SS"]

            def front_stats(j):
                par = j % 2

                @block.tensor
                def _(t, j=j, par=par, pvx=prev_vxn2[par]):
                    C.wait(t, "SS", ss_sq_arr[j])
                    C.wait(t, f"LDX{par}", ld_x_arr[j])
                    if j == 0:
                        C.wait(t, "VV", vv_const)
                        C.wait(t, "WQC", wq_loads)
                    if pvx > 0:
                        C.wait(t, "VV", pvx)
                    xb3 = xTb2[par][:].rearrange("p (a n) -> p a n", a=8)
                    sq3 = xsq2[par][:].rearrange("p (a n) -> p a n", a=8)
                    xs3 = xTs[par][:].rearrange("p (a n) -> p a n", a=8)
                    gw3 = gwT_sb[:].rearrange("p (a n) -> p a n", a=8)
                    for k in range(8):
                        t.matmul(out=pS[par][:, 0:128], lhsT=ones_b[:], rhs=xb3[:, k, :],
                                 start=(k == 0), stop=(k == 7))
                    for k in range(8):
                        t.matmul(out=pS[par][:, 128:256], lhsT=ones_b[:], rhs=sq3[:, k, :],
                                 start=(k == 0), stop=(k == 7))
                    last = None
                    for k in range(8):
                        last = t.matmul(out=pS[par][:, 288:288 + NE], lhsT=xs3[:, k, :],
                                        rhs=gw3[:, k, :], start=(k == 0), stop=(k == 7))
                    C.inc(last, "TT")
                tt_stats_arr[j] = C.v["TT"]
                xt_free_tt[par] = C.v["TT"]

            front_dmas(0)
            front_sq(0)
            front_stats(0)

            for jj in range(NJ):
                par = jj % 2
                cpos = jj % 4
                tt_stats = tt_stats_arr[jj]

                if jj + 1 < NJ:
                    front_dmas(jj + 1)

                @block.vector
                def _(v, par=par):
                    C.wait(v, "TT", tt_stats)
                    v.tensor_scalar_mul(out=m_f[:], in0=pS[par][:, 0:128], scalar1=1.0 / D)
                    v.tensor_scalar_mul(out=msq_f[:], in0=pS[par][:, 128:256], scalar1=1.0 / D)
                    v.tensor_tensor(out=m2_f[:], in0=m_f[:], in1=m_f[:], op=ALU.mult)
                    i = v.tensor_tensor(out=var_f[:], in0=msq_f[:], in1=m2_f[:], op=ALU.subtract)
                    C.inc(i, "VV")
                vv_st1 = C.v["VV"]

                @block.scalar
                def _(s):
                    C.wait(s, "VV", vv_st1)
                    s.activation(out=rstd_f[:], in_=var_f[:], func=AF.Ln, bias=eps_sb[:])
                    i = s.activation(out=rstd_f[:], in_=rstd_f[:], func=AF.Exp, scale=-0.5)
                    C.inc(i, "SS")
                ss_rstd = C.v["SS"]
                if jj + 1 < NJ:
                    front_sq(jj + 1)

                @block.tensor
                def _(t, par=par):
                    C.wait(t, "SS", ss_rstd)
                    t.matmul(out=pS[par][:, 320:321], lhsT=m_f[:], rhs=ident_f[:, 0:1],
                             start=True, stop=True)
                    i = t.matmul(out=pS[par][:, 321:322], lhsT=rstd_f[:], rhs=ident_f[:, 0:1],
                                 start=True, stop=True)
                    C.inc(i, "TT")
                tt_t1b = C.v["TT"]
                if jj + 1 < NJ:
                    front_stats(jj + 1)

                @block.vector
                def _(v, jj=jj, par=par, cpos=cpos,
                      sdq=stage_sdp[(jj // 2) % 2], tt_t1b=tt_t1b):
                    C.wait(v, "TT", tt_t1b)
                    if cpos % 2 == 0 and sdq > 0:
                        C.wait(v, f"SDAQ{(jj // 2) % 2}", sdq)
                    v.tensor_scalar_mul(out=sm_t[:, 0:1], in0=pS[par][:, 320:321], scalar1=1.0)
                    v.tensor_scalar_mul(out=sm_t[:, 5:6], in0=pS[par][:, 321:322], scalar1=-1.0)
                    v.scalar_tensor_tensor(out=lg1[:], in0=gcb[:], scalar=sm_t[:, 0:1],
                                           in1=pS[par][:, 288:288 + NE], op0=ALU.mult, op1=ALU.subtract)
                    v.tensor_scalar_mul(out=logits_all[:, jj * NE:(jj + 1) * NE], in0=lg1[:],
                                        scalar1=sm_t[:, 5:6])
                    v.tensor_copy(out=m_b[:], in_=m_f[:])
                    xb3 = xTb2[par][:].rearrange("p (a n) -> p a n", a=8)
                    xn3 = xnstage[(jj // 2) % 2][:].rearrange("p (a n) -> p a n", a=8)
                    mb_b = m_b[:].unsqueeze(1).broadcast_to([128, 8, 128])
                    rs_b = rstd_f[:].unsqueeze(1).broadcast_to([128, 8, 128])
                    v.tensor_tensor(out=xb3, in0=xb3, in1=mb_b, op=ALU.subtract)
                    last = v.tensor_tensor(out=xn3[:, :, (cpos % 2) * 128:(cpos % 2 + 1) * 128],
                                           in0=xb3, in1=rs_b, op=ALU.mult)
                    C.inc(last, "VV")
                vv_xn = C.v["VV"]
                prev_vxn = vv_xn
                prev_vxn2[jj % 2] = vv_xn

                def proj(t, wsb, cpos, bank, jj=jj):
                    xn3 = xnstage[(jj // 2) % 2][:].rearrange("p (a n) -> p a n", a=8)
                    w3 = wsb[:].rearrange("p (a n) -> p a n", a=8)
                    last = None
                    for k in range(8):
                        lh = xn3[:, k, (cpos % 2) * 128:(cpos % 2 + 1) * 128]
                        t.matmul(out=pP[bank][:, 0:512], lhsT=lh, rhs=w3[:, k, 0:512],
                                 start=(k == 0), stop=(k == 7))
                        last = t.matmul(out=pP[bank][:, 512:1024], lhsT=lh, rhs=w3[:, k, 512:1024],
                                        start=(k == 0), stop=(k == 7))
                    return last

                @block.tensor
                def _(t, cpos=cpos, jj=jj):
                    C.wait(t, "VV", vv_xn)
                    if jj == 0:
                        C.wait(t, "WQC", wq_loads)
                    C.inc(proj(t, wq_b, cpos, 0), "TT")
                    C.inc(proj(t, wk_b, cpos, 1), "TT")
                tt_q = C.v["TT"] - 1
                tt_k = C.v["TT"]

                def rope_emit(v, dst, jj, bank):
                    ra = ropeA_sb[:, jj * 64:(jj + 1) * 64]
                    rb = ropeB_sb[:, jj * 64:(jj + 1) * 64]
                    ra_b = ra.unsqueeze(1).broadcast_to([128, 16, HD])
                    p3 = pP[bank][:].rearrange("p (h d) -> p h d", h=16)
                    d2 = dst[:]
                    v.tensor_tensor(out=d2.rearrange("p (h d) -> p h d", h=16),
                                    in0=p3, in1=ra_b, op=ALU.mult)
                    p4 = pP[bank][:].rearrange("p (h d two) -> p h d two", h=16, two=2)
                    rb4 = rb.rearrange("p (d two) -> p d two", two=2)
                    t4 = rtmp[:].rearrange("p (h d two) -> p h d two", h=16, two=2)
                    v.tensor_tensor(out=t4[:, :, :, 0:1], in0=p4[:, :, :, 1:2],
                                    in1=rb4[:, :, 0:1].unsqueeze(1).broadcast_to([128, 16, 32, 1]),
                                    op=ALU.mult)
                    v.tensor_tensor(out=t4[:, :, :, 1:2], in0=p4[:, :, :, 0:1],
                                    in1=rb4[:, :, 1:2].unsqueeze(1).broadcast_to([128, 16, 32, 1]),
                                    op=ALU.mult)
                    return v.tensor_tensor(out=d2, in0=d2, in1=rtmp[:], op=ALU.add)

                @block.vector
                def _(v, jj=jj):
                    C.wait(v, "TT", tt_q)
                    C.inc(rope_emit(v, qr_sb, jj, 0), "VV")
                vv_rq = C.v["VV"]

                @block.vector
                def _(v, jj=jj):
                    C.wait(v, "TT", tt_k)
                    C.inc(rope_emit(v, kr_sb, jj, 1), "VV")
                vv_rk = C.v["VV"]

                @block.tensor
                def _(t, cpos=cpos):
                    C.wait(t, "VV", vv_rq)
                    C.inc(proj(t, wv_b, cpos, 0), "TT")
                tt_v = C.v["TT"]

                @block.vector
                def _(v, par=par, sdv=vdma_sd[par]):
                    C.wait(v, "TT", tt_v)
                    if sdv > 0:
                        C.wait(v, f"SDV{par}", sdv)
                    i = v.tensor_copy(out=vr_sb[par][:], in_=pP[0][:])
                    C.inc(i, "VV")
                vv_vr = C.v["VV"]

                @block.sync
                def _(sy, jj=jj, par=par):
                    C.wait(sy, "VV", vv_vr)
                    i = sy.dma_start(
                        out=Vn_s.rearrange("h (tc p) d -> tc p h d", p=128)[jj],
                        in_=vr_sb[par][:].rearrange("p (h d) -> p h d", h=16))
                    C.inc(i, f"SDV{par}", 16)
                vdma_sd[par] = C.v[f"SDV{par}"]

                srcs = [qr_sb, kr_sb]
                ss0 = C.v["SS"]
                evac_pred = [ss0 + i + 1 for i in range(4)]
                round_tt = []

                @block.tensor
                def _(t, pt_free_ss=list(pt_free_ss)):
                    C.wait(t, "VV", vv_vr)
                    for r in range(4):
                        if r >= 2:
                            C.wait(t, "SS", evac_pred[r - 2])
                        elif pt_free_ss[r % 2] > 0:
                            C.wait(t, "SS", pt_free_ss[r % 2])
                        src = srcs[r // 2]
                        last = None
                        for q in range(4):
                            dd = (r % 2) * 4 + q
                            last = t.transpose(out=pT[r % 2][:, q * 128:(q + 1) * 128],
                                               in_=src[:, dd * 128:(dd + 1) * 128],
                                               identity=ident_b[:])
                        C.inc(last, "TT")
                        round_tt.append(C.v["TT"])

                @block.scalar
                def _(s, cpos=cpos, jj=jj, sdq=stage_sdp[(jj // 2) % 2]):
                    stages = [qstage[(jj // 2) % 2], kstage[(jj // 2) % 2]]
                    if cpos % 2 == 0 and sdq > 0:
                        C.wait(s, f"SDAQ{(jj // 2) % 2}", sdq)
                    for r in range(4):
                        C.wait(s, "TT", round_tt[r])
                        stg = stages[r // 2]
                        dst = stg[:].rearrange("p (a j t) -> p a j t", a=8, j=2)[:, (r % 2) * 4:(r % 2) * 4 + 4, cpos % 2, :]
                        i = s.copy(out=dst, in_=pT[r % 2][:].rearrange("p (a t) -> p a t", a=4))
                        C.inc(i, "SS")
                assert C.v["SS"] == evac_pred[-1]
                pt_free_ss = [evac_pred[2], evac_pred[3]]

                if cpos % 2 == 1:
                    ss_now = C.v["SS"]
                    vv_now = vv_xn
                    hg = jj // 2

                    @block.sync
                    def _(sy, hg=hg, ss_now=ss_now, vv_now=vv_now):
                        P = hg % 2
                        C.wait(sy, "SS", ss_now)
                        for stg, dst in ((qstage[P], QT_s), (kstage[P], KT_s)):
                            i = sy.dma_start(
                                out=dst.rearrange("(a p) n -> p a n", p=128)[:, :, hg * 256:(hg + 1) * 256],
                                in_=stg[:].rearrange("p (a n) -> p a n", a=8))
                            C.inc(i, f"SDAQ{P}", 16)
                        C.wait(sy, "VV", vv_now)
                        i = sy.dma_start(
                            out=xnT_s.rearrange("(a p) n -> p a n", p=128)[:, :, hg * 256:(hg + 1) * 256],
                            in_=xnstage[P][:].rearrange("p (a n) -> p a n", a=8))
                        C.inc(i, f"SDAQ{P}", 16)
                    stage_sdp[hg % 2] = C.v[f"SDAQ{hg % 2}"]

            @block.scalar
            def _(s, vxn=prev_vxn):
                C.wait(s, "VV", vxn)
                i = s.activation(out=scores_sb[:], in_=logits_all[:], func=AF.Sigmoid)
                C.inc(i, "SS")
            ss_sig = C.v["SS"]

            @block.vector
            def _(v):
                C.wait(v, "SS", ss_sig)

                def tree(out3, in3, B, op):
                    cur, width, lvl = in3, B, 0
                    while width > 2:
                        half = width // 2
                        A = cur.shape[1]
                        dst = tscr[lvl % 2][:].rearrange("p (a w) -> p a w", a=A)[:, :, 0:half]
                        v.tensor_tensor(out=dst, in0=cur[:, :, 0:half],
                                        in1=cur[:, :, half:width], op=op)
                        cur, width, lvl = dst, half, lvl + 1
                    v.tensor_tensor(out=out3, in0=cur[:, :, 0:1], in1=cur[:, :, 1:2], op=op)

                sfc4 = sfc_sb[:].rearrange("p (c g s) -> p c g s", c=32, g=8)
                sc4 = scores_sb[:].rearrange("p (c g s) -> p c g s", c=32, g=8)
                ebb = eb_bc[:].rearrange("p (g s) -> p g s", g=8).unsqueeze(1).broadcast_to([128, 32, 8, 4])
                v.tensor_tensor(out=sfc4, in0=sc4, in1=ebb, op=ALU.add)
                g3 = gwork[:].rearrange("p (c g) -> p c g", c=32)
                sfc3 = sfc_sb[:].rearrange("p (cg s) -> p cg s", s=4)
                tree(g3.rearrange("p c g -> p (c g)").rearrange("p (cg o) -> p cg o", o=1),
                     sfc3, 4, ALU.max)
                v.memset(gmask[:], 0.0)
                for it in range(4):
                    rm3 = rmax[:].rearrange("p (c o) -> p c o", o=1)
                    tree(rm3, g3, 8, ALU.max)
                    rb_ = rmax[:].unsqueeze(2).broadcast_to([128, 32, 8])
                    iw3 = isw[:, 0:256].rearrange("p (c g) -> p c g", c=32)
                    v.tensor_tensor(out=iw3, in0=g3, in1=rb_, op=ALU.is_equal)
                    v.tensor_tensor(out=gmask[:], in0=gmask[:], in1=isw[:, 0:256], op=ALU.add)
                    v.scalar_tensor_tensor(out=gwork[:], in0=isw[:, 0:256], scalar=-1e9,
                                           in1=gwork[:], op0=ALU.mult, op1=ALU.add)
                gm4 = gmask[:].rearrange("p (c g) -> p c g", c=32).unsqueeze(3).broadcast_to([128, 32, 8, 4])
                msk4 = isw[:].rearrange("p (c g s) -> p c g s", c=32, g=8)
                v.tensor_tensor(out=msk4, in0=sfc4, in1=gm4, op=ALU.mult)
                v.memset(m6[:], 0.0)
                w3 = isw[:].rearrange("p (c e) -> p c e", c=32)
                for it in range(6):
                    rm3 = rmax[:].rearrange("p (c o) -> p c o", o=1)
                    tree(rm3, w3, NE, ALU.max)
                    rb_ = rmax[:].unsqueeze(2).broadcast_to([128, 32, NE])
                    tw3 = tw[:].rearrange("p (c e) -> p c e", c=32)
                    v.tensor_tensor(out=tw3, in0=w3, in1=rb_, op=ALU.is_equal)
                    v.tensor_tensor(out=m6[:], in0=m6[:], in1=tw[:], op=ALU.add)
                    v.scalar_tensor_tensor(out=isw[:], in0=tw[:], scalar=-1e9,
                                           in1=isw[:], op0=ALU.mult, op1=ALU.add)
                v.tensor_tensor(out=tw[:], in0=scores_sb[:], in1=m6[:], op=ALU.mult)
                t3 = tsum[:].rearrange("p (c o) -> p c o", o=1)
                tree(t3, tw[:].rearrange("p (c e) -> p c e", c=32), NE, ALU.add)
                v.reciprocal(out=rr[:], in_=tsum[:])
                rrb = rr[:].unsqueeze(2).broadcast_to([128, 32, NE])
                i = v.scalar_tensor_tensor(out=c_nat[:].rearrange("p (c e) -> p c e", c=32),
                                           in0=tw[:].rearrange("p (c e) -> p c e", c=32),
                                           scalar=2.5, in1=rrb, op0=ALU.mult, op1=ALU.mult)
                C.inc(i, "VV")
            vv_cnat = C.v["VV"]

            if ("cnat" in DEBUG_DUMP) or ("logits" in DEBUG_DUMP):
                @block.sync
                def _(sy):
                    C.wait(sy, "VV", vv_cnat)
                    if "cnat" in DEBUG_DUMP:
                        i = sy.dma_start(out=cnat_d, in_=c_nat[:])
                        C.inc(i, "SDD", 16)
                    if "logits" in DEBUG_DUMP:
                        i = sy.dma_start(out=logits_d, in_=logits_all[:])
                        C.inc(i, "SDD", 16)

            fenceA = C.snap()

        with contextlib.ExitStack() as pb:
            Eb = pb.enter_context
            qhp = [Eb(nc.sbuf_tensor(f"qhp{i}", [128, S], BF16)) for i in range(2)]
            khp = [Eb(nc.sbuf_tensor(f"khp{i}", [128, S], BF16)) for i in range(2)]
            vnp = [[Eb(nc.sbuf_tensor(f"vnp{i}{j}", [128, 32 * 65], BF16)) for j in range(2)]
                   for i in range(2)]
            pt_sb = [Eb(nc.sbuf_tensor(f"ptb{i}", [128, 512], BF16)) for i in range(4)]
            dmask_sb = Eb(nc.sbuf_tensor("dmask_sb", [128, 4 * 512], BF16))
            r_sb = [Eb(nc.sbuf_tensor(f"r_sb{i}", [1, 512], F32)) for i in range(2)]
            ctxs_sb = [Eb(nc.sbuf_tensor(f"ctxs_sb{i}", [64, 512], BF16)) for i in range(2)]
            ctx_stage = [Eb(nc.sbuf_tensor(f"ctx_stage{i}", [64, S], BF16)) for i in range(2)]

            psc = [Eb(nc.psum_tensor(f"psc{i}", [128, 512], F32)) for i in range(6)]
            pctx = [Eb(nc.psum_tensor(f"pctx{i}", [65, 512], F32)) for i in range(2)]

            FENCE_NAMES = ("LDC", "LDX0", "LDX1", "WQC", "WQB", "WQW", "WQE0",
                           "WQE1", "SDA", "SDAQ0", "SDAQ1", "LDH0", "LDH1",
                           "SDB0", "SDB1", "LDN",
                           "LCC0", "LCC1", "LXR0", "LXR1", "SDV0", "SDV1", "SDD",
                           "TT", "SS", "VV", "GG")

            @block.gpsimd
            def _(g):
                for nm in FENCE_NAMES:
                    C.wait(g, nm, fenceA[nm])
                i = g.dma_start(out=dmask_sb[:].rearrange("p (j n) -> p j n", j=4),
                                in_=dmask_d.rearrange("j p n -> p j n"))
                C.inc(i, "WQB", 16)
                i = g.dma_start(out=wo_b[:].rearrange("p (a n) -> p a n", a=8),
                                in_=wo_d.rearrange("(a p) n -> p a n", p=128))
                C.inc(i, "WQW", 16)
                i = g.dma_start(out=gub0[:].rearrange("p (a n) -> p a n", a=8),
                                in_=guT_d[0:1, :, :].rearrange("e (a p) n -> p (e a) n", p=128))
                C.inc(i, "WQE0", 16)
                i = g.dma_start(out=dnb0[:].rearrange("p (a n) -> p a n", a=4),
                                in_=dnT_d[0:1, :, :].rearrange("e (a p) n -> p (e a) n", p=128))
                C.inc(i, "WQE0", 16)
                for i2 in range(2):
                    for j2 in range(2):
                        last = g.memset(vnp[i2][j2][:].rearrange("p (kk o) -> p kk o", o=65)[:, :, 64:65], 1.0)
                C.inc(last, "GG")
            wq_mask = C.v["WQB"]
            gg_ones = C.v["GG"]

            pair_t_end = [0, 0]
            stage_sd_b = [0, 0]
            rb_free_vv = [0, 0]
            ctxs_free_vv = [0, 0]
            pctx_free = [dict(ss=0, vv=0), dict(ss=0, vv=0)]
            psc_carry = [0] * 6
            ptb_carry = [0, 0, 0, 0]
            pending = [None]
            head_mult_vv = {}

            def emit_replicates(t, pend):
                for s_ in range(2):
                    C.wait(t, "SS", pend["ss_evac"][s_])
                    C.wait(t, "VV", pend["vv_recip"][s_])
                    if pend["rb_free"][s_] > 0:
                        C.wait(t, "VV", pend["rb_free"][s_])
                    i = t.matmul(out=pctx[s_][0:64, :], lhsT=ones1_f[0:1, 0:64],
                                 rhs=r_sb[s_][:], start=True, stop=True)
                    C.inc(i, "TT")
                pend["tt_rb"] = C.v["TT"]

            def flush_norm():
                pend = pending[0]
                if pend is None:
                    return
                assert pend["tt_rb"] is not None
                p_, qi = pend["p"], pend["qi"]

                @block.vector
                def _(v, pend=pend, p_=p_, qi=qi,
                      sd0=stage_sd_b[0], sd1=stage_sd_b[1]):
                    C.wait(v, "TT", pend["tt_rb"])
                    for s_ in range(2):
                        C.wait(v, "SS", pend["ss_evac"][s_])
                        if qi == 0 and (sd0 if s_ == 0 else sd1) > 0:
                            C.wait(v, f"SDB{s_}", sd0 if s_ == 0 else sd1)
                        i = v.tensor_tensor(out=ctx_stage[s_][:, qi * 512:(qi + 1) * 512],
                                            in0=ctxs_sb[s_][:], in1=pctx[s_][0:64, :], op=ALU.mult)
                        C.inc(i, "VV")
                        rb_free_vv[s_] = C.v["VV"]
                        ctxs_free_vv[s_] = C.v["VV"]
                        head_mult_vv[2 * p_ + s_] = C.v["VV"]
                pending[0] = None

            def emit_stage_dmas(p_):
                for s_ in range(2):
                    h = 2 * p_ + s_

                    @block.sync
                    def _(sy, h=h, s_=s_, vvh=head_mult_vv[h]):
                        C.wait(sy, "VV", vvh)
                        i = sy.dma_start(out=ctxT_s[h * 64:(h + 1) * 64, :],
                                         in_=ctx_stage[s_][:])
                        C.inc(i, f"SDB{s_}", 16)
                    stage_sd_b[s_] = C.v[f"SDB{s_}"]

            for p in range(NH // 2):
                par = p % 2
                hA, hB = 2 * p, 2 * p + 1

                @block.sync
                def _(sy, p=p, par=par, hA=hA, hB=hB):
                    if p == 0:
                        for nm in FENCE_NAMES:
                            C.wait(sy, nm, fenceA[nm])
                    C.wait(sy, "TT", pair_t_end[par])
                    i = sy.dma_start(out=qhp[par][:], in_=QT_s[hA * 64:hA * 64 + 128, :])
                    C.inc(i, f"LDH{par}", 16)
                    i = sy.dma_start(out=khp[par][:], in_=KT_s[hA * 64:hA * 64 + 128, :])
                    C.inc(i, f"LDH{par}", 16)
                    if p == 0:
                        C.wait(sy, "GG", gg_ones)
                    for s_, h in ((0, hA), (1, hB)):
                        i = sy.dma_start(
                            out=vnp[par][s_][:].rearrange("p (kk o) -> p kk o", o=65)[:, :, 0:64],
                            in_=Vn_s[h].rearrange("(kk p) d -> p kk d", p=128))
                        C.inc(i, f"LDH{par}", 16)
                head_ld = C.v[f"LDH{par}"]

                if pending[0] is not None:
                    @block.tensor
                    def _(t, pend=pending[0]):
                        emit_replicates(t, pend)
                    flush_norm()
                if p >= 1:
                    emit_stage_dmas(p - 1)

                for qi in range(8):
                    nk = 4 * (qi + 1)
                    t0 = C.v["TT"]
                    s0 = C.v["SS"]
                    v0 = C.v["VV"]
                    pend = pending[0]
                    v_base = v0 + (2 if pend is not None else 0)

                    order = []
                    for kc in range(nk):
                        order.append(("sA", kc))
                        order.append(("sB", kc))
                        if kc >= 1:
                            order.append(("cA", kc - 1))
                            order.append(("cB", kc - 1))
                    order.append(("cA", nk - 1))
                    order.append(("cB", nk - 1))
                    defer_at = 4 if pend is not None else None
                    pos = {}
                    idx_tt = t0
                    for idx, op in enumerate(order):
                        if defer_at is not None and idx == defer_at:
                            idx_tt += 2
                        idx_tt += 1
                        pos[op] = idx_tt
                    diag = list(range(nk - 4, nk))
                    pt_ready = {}
                    for kc in range(nk):
                        if kc in diag:
                            j = diag.index(kc)
                            pt_ready[("A", kc)] = ("VV", v_base + 2 * j + 1)
                            pt_ready[("B", kc)] = ("VV", v_base + 2 * j + 2)
                        else:
                            pt_ready[("A", kc)] = ("SS", s0 + 2 * kc + 1)
                            pt_ready[("B", kc)] = ("SS", s0 + 2 * kc + 2)

                    @block.tensor
                    def _(t, p=p, par=par, qi=qi, nk=nk, order=order, pos=pos,
                          pt_ready=pt_ready, s0=s0, defer_at=defer_at, pend=pend,
                          pscc=list(psc_carry), pctxf=[dict(x) for x in pctx_free],
                          v0=v0, hld=head_ld):
                        if qi == 0:
                            C.wait(t, f"LDH{par}", hld)
                            if p == 0:
                                C.wait(t, "WQB", wq_mask)
                        for idx, (kind, kc) in enumerate(order):
                            if defer_at is not None and idx == defer_at:
                                emit_replicates(t, pend)
                            s_ = 0 if kind[1] == "A" else 1
                            if kind[0] == "s":
                                slot = 3 * s_ + kc % 3
                                if kc >= 3:
                                    C.wait(t, "SS", s0 + 2 * (kc - 3) + 1 + s_)
                                elif pscc[slot] > 0:
                                    C.wait(t, "SS", pscc[slot])
                                i = t.matmul(out=psc[slot][:],
                                             lhsT=khp[par][s_ * 64:s_ * 64 + 64, kc * 128:(kc + 1) * 128],
                                             rhs=qhp[par][s_ * 64:s_ * 64 + 64, qi * 512:(qi + 1) * 512],
                                             start=True, stop=True)
                                C.inc(i, "TT")
                            else:
                                nm, val = pt_ready[("A" if s_ == 0 else "B", kc)]
                                C.wait(t, nm, val)
                                if kc == 0:
                                    if pctxf[s_]["ss"] > 0:
                                        C.wait(t, "SS", pctxf[s_]["ss"])
                                    if pctxf[s_]["vv"] > 0:
                                        C.wait(t, "VV", pctxf[s_]["vv"])
                                    if pend is not None:
                                        C.wait(t, "VV", v0 + 1 + s_)
                                vn3 = vnp[par][s_][:].rearrange("p (kk o) -> p kk o", o=65)
                                i = t.matmul(out=pctx[s_][:],
                                             lhsT=vn3[:, kc, :],
                                             rhs=pt_sb[2 * s_ + kc % 2][:],
                                             start=(kc == 0), stop=(kc == nk - 1))
                                C.inc(i, "TT")
                    flush_norm()
                    assert C.v["TT"] == pos[("cB", nk - 1)], (C.v["TT"], pos)
                    pair_t_end[par] = C.v["TT"]
                    for kc in range(nk):
                        psc_carry[3 * 0 + kc % 3] = s0 + 2 * kc + 1
                        psc_carry[3 * 1 + kc % 3] = s0 + 2 * kc + 2

                    @block.scalar
                    def _(s, nk=nk, pos=pos, s0=s0, ptbc=list(ptb_carry),
                          ctxsf=list(ctxs_free_vv)):
                        for kc in range(nk):
                            for s_ in range(2):
                                slot = 2 * s_ + kc % 2
                                C.wait(s, "TT", pos[("sA" if s_ == 0 else "sB", kc)])
                                if kc >= 2:
                                    C.wait(s, "TT", pos[("cA" if s_ == 0 else "cB", kc - 2)])
                                elif ptbc[slot] > 0:
                                    C.wait(s, "TT", ptbc[slot])
                                i = s.activation(out=pt_sb[slot][:],
                                                 in_=psc[3 * s_ + kc % 3][:],
                                                 func=AF.Exp, scale=0.125)
                                C.inc(i, "SS")
                        for s_ in range(2):
                            C.wait(s, "TT", pos[("cA" if s_ == 0 else "cB", nk - 1)])
                            if ctxsf[s_] > 0:
                                C.wait(s, "VV", ctxsf[s_])
                            i = s.copy(out=ctxs_sb[s_][:], in_=pctx[s_][0:64, :])
                            C.inc(i, "SS")
                    assert C.v["SS"] == s0 + 2 * nk + 2
                    for kc in range(nk):
                        ptb_carry[2 * 0 + kc % 2] = pos[("cA", kc)]
                        ptb_carry[2 * 1 + kc % 2] = pos[("cB", kc)]
                    pctx_free[0] = dict(ss=s0 + 2 * nk + 1, vv=0)
                    pctx_free[1] = dict(ss=s0 + 2 * nk + 2, vv=0)

                    dm3 = dmask_sb[:].rearrange("p (j n) -> p j n", j=4)

                    @block.vector
                    def _(v, nk=nk, diag=diag, s0=s0, pos=pos):
                        for j, kc in enumerate(diag):
                            for s_ in range(2):
                                C.wait(v, "SS", s0 + 2 * kc + 1 + s_)
                                i = v.tensor_tensor(out=pt_sb[2 * s_ + kc % 2][:],
                                                    in0=pt_sb[2 * s_ + kc % 2][:],
                                                    in1=dm3[:, j, :], op=ALU.mult)
                                C.inc(i, "VV")
                        for s_ in range(2):
                            C.wait(v, "TT", pos[("cA" if s_ == 0 else "cB", nk - 1)])
                            i = v.reciprocal(out=r_sb[s_][:], in_=pctx[s_][64:65, :])
                            C.inc(i, "VV")
                    assert C.v["VV"] == v_base + 10
                    vv_recips = (v_base + 9, v_base + 10)
                    pctx_free[0]["vv"] = v_base + 9
                    pctx_free[1]["vv"] = v_base + 10

                    pending[0] = dict(p=p, qi=qi,
                                      vv_recip=vv_recips,
                                      rb_free=tuple(rb_free_vv),
                                      ss_evac=(s0 + 2 * nk + 1, s0 + 2 * nk + 2),
                                      tt_rb=None)

            @block.tensor
            def _(t, pend=pending[0]):
                emit_replicates(t, pend)
            flush_norm()
            emit_stage_dmas(NH // 2 - 1)

            fenceB = C.snap()
        with contextlib.ExitStack() as pc:
            Ec = pc.enter_context
            acc = Ec(nc.sbuf_tensor("acc", [128, 16 * 1024], F32))
            xnTh = Ec(nc.sbuf_tensor("xnTh", [128, 8 * 2048], BF16))
            gub = [gub0, Ec(nc.sbuf_tensor("gub1", [128, 8 * 1024], BF16))]
            dnb = [dnb0, Ec(nc.sbuf_tensor("dnb1", [128, 4 * 1024], BF16))]
            ctxc = [Ec(nc.sbuf_tensor(f"ctxc{i}", [128, 8 * 128], BF16)) for i in range(2)]
            act_sb = Ec(nc.sbuf_tensor("act_sb", [128, 4 * 512], BF16))
            silu_sb = [Ec(nc.sbuf_tensor(f"silu{i}", [128, 512], BF16)) for i in range(2)]
            xres = [Ec(nc.sbuf_tensor(f"xres{i}", [128, 1024], F32)) for i in range(2)]
            outb = [Ec(nc.sbuf_tensor(f"outb{i}", [128, 1024], F32)) for i in range(2)]

            pg = [Ec(nc.psum_tensor(f"pg{i}", [128, 512], F32)) for i in range(2)]
            pu = [Ec(nc.psum_tensor(f"pu{i}", [128, 512], F32)) for i in range(2)]
            pd = [Ec(nc.psum_tensor(f"pd{i}", [128, 1024], F32)) for i in range(2)]

            @block.gpsimd
            def _(g):
                for nm in ("LDC", "LDX0", "LDX1", "WQC", "WQB", "WQW", "WQE0", "WQE1", "SDA", "SDAQ0", "SDAQ1", "LDH0", "LDH1", "SDB0", "SDB1", "LDN", "LCC0", "LCC1", "LXR0", "LXR1", "SDV0", "SDV1", "WQX0", "WQX1", "SDD", "TT", "SS", "VV", "GG"):
                    C.wait(g, nm, fenceB[nm])
            wq_wo = C.v["WQW"]

            half_t_end = 0
            wbuf_free_tt = [0, 0]
            pd_free_vv = [0, 0]
            pg_free_ss = [0, 0]
            pu_free_vv = [0, 0]
            act_free_tt = 0
            out_done_od = 0
            xres_free_vv = [0, 0]
            outb_free_od = [0, 0]

            wo_last = [0, 0]
            for hf in range(2):
                wo_mm_done = {}
                ctx_ld_v = [0, 0]

                def emit_ctx_load(jj, hf=hf):
                    @block.scalar
                    def _(se, jj=jj, hf=hf, wol=wo_last[jj % 2]):
                        if hf == 0 and jj == 0:
                            for nm in ("LDC", "LDX0", "LDX1", "WQC", "WQB", "WQW", "WQE0", "WQE1", "SDA", "SDAQ0", "SDAQ1", "LDH0", "LDH1", "SDB0", "SDB1", "LDN", "LCC0", "LCC1", "LXR0", "LXR1", "SDV0", "SDV1", "WQX0", "WQX1", "SDD", "TT", "SS", "VV", "GG"):
                                C.wait(se, nm, fenceB[nm])
                        if jj >= 2:
                            C.wait(se, "TT", wo_mm_done[jj - 2])
                        elif wol > 0:
                            C.wait(se, "TT", wol)
                        i = se.dma_start(
                            out=ctxc[jj % 2][:].rearrange("p (a n) -> p a n", a=8),
                            in_=ctxT_s.rearrange("(a p) n -> p a n", p=128)[:, :, (hf * 16 + jj) * 128:(hf * 16 + jj + 1) * 128])
                        C.inc(i, f"LCC{jj % 2}", 16)
                    ctx_ld_v[jj % 2] = C.v[f"LCC{jj % 2}"]

                emit_ctx_load(0)
                emit_ctx_load(1)

                @block.scalar
                def _(se, hf=hf, half_t_end=half_t_end):
                    C.wait(se, "TT", half_t_end)
                    i = se.dma_start(
                        out=xnTh[:].rearrange("p (a n) -> p a n", a=8),
                        in_=xnT_s.rearrange("(a p) n -> p a n", p=128)[:, :, hf * 2048:(hf + 1) * 2048])
                    C.inc(i, "LDN", 16)
                ld_xn = C.v["LDN"]

                for jj in range(16):
                    if jj >= 2:
                        emit_ctx_load(jj)

                    @block.tensor
                    def _(t, jj=jj, pdf=pd_free_vv[jj % 2], ldv=ctx_ld_v[jj % 2]):
                        C.wait(t, f"LCC{jj % 2}", ldv)
                        if jj == 0:
                            C.wait(t, "WQW", wq_wo)
                        if pdf > 0:
                            C.wait(t, "VV", pdf)
                        c3 = ctxc[jj % 2][:].rearrange("p (a n) -> p a n", a=8)
                        w3 = wo_b[:].rearrange("p (a n) -> p a n", a=8)
                        last = None
                        for k in range(8):
                            t.matmul(out=pd[jj % 2][:, 0:512], lhsT=c3[:, k, :],
                                     rhs=w3[:, k, 0:512], start=(k == 0), stop=(k == 7))
                            last = t.matmul(out=pd[jj % 2][:, 512:1024], lhsT=c3[:, k, :],
                                            rhs=w3[:, k, 512:1024], start=(k == 0), stop=(k == 7))
                        C.inc(last, "TT")
                    wo_mm_done[jj] = C.v["TT"]

                    @block.vector
                    def _(v, jj=jj, hf=hf):
                        C.wait(v, "TT", wo_mm_done[jj])
                        i = v.tensor_tensor(out=acc[:, jj * 1024:(jj + 1) * 1024],
                                            in0=pd[jj % 2][:], in1=bo_bc[:], op=ALU.add)
                        C.inc(i, "VV")
                    pd_free_vv[jj % 2] = C.v["VV"]
                wo_last = [wo_mm_done[14], wo_mm_done[15]]

                for e in range(34):
                    par = e % 2

                    if hf == 0 and e == 0:
                        wq_e = C.v["WQE0"]
                        t0 = C.v["TT"]
                        s0 = C.v["SS"]
                        v0 = C.v["VV"]
                    if not (hf == 0 and e == 0):
                        pass

                    @block.gpsimd
                    def _(g, e=e, par=par, hf=hf, wfree=wbuf_free_tt[par]):
                        if hf == 0 and e == 0:
                            return
                        C.wait(g, "TT", wfree)
                        WQE = f"WQE{par}"
                        if e < NE:
                            i = g.dma_start(out=gub[par][:].rearrange("p (a n) -> p a n", a=8),
                                            in_=guT_d[e:e + 1, :, :].rearrange("e (a p) n -> p (e a) n", p=128))
                            C.inc(i, f"WQE{par}", 16)
                            i = g.dma_start(out=dnb[par][:].rearrange("p (a n) -> p a n", a=4),
                                            in_=dnT_d[e:e + 1, :, :].rearrange("e (a p) n -> p (e a) n", p=128))
                            C.inc(i, f"WQE{par}", 16)
                        else:
                            si = e - NE
                            g3 = gub[par][:].rearrange("p (a n) -> p a n", a=8)
                            i = g.dma_start(out=g3[:, :, 0:512],
                                            in_=sgwT_d[:, si * 512:(si + 1) * 512].rearrange("(a p) n -> p a n", p=128))
                            C.inc(i, f"WQE{par}", 16)
                            i = g.dma_start(out=g3[:, :, 512:1024],
                                            in_=suwT_d[:, si * 512:(si + 1) * 512].rearrange("(a p) n -> p a n", p=128))
                            C.inc(i, f"WQE{par}", 16)
                            i = g.dma_start(out=dnb[par][:].rearrange("p (a n) -> p a n", a=4),
                                            in_=sdwT_d[si * 512:(si + 1) * 512, :].rearrange("(a p) n -> p a n", p=128))
                            C.inc(i, f"WQE{par}", 16)
                    wq_e = C.v[f"WQE{par}"]

                    t0 = C.v["TT"]
                    s0 = C.v["SS"]
                    v0 = C.v["VV"]
                    gpos = lambda blk, ip: t0 + blk * 12 + 2 * ip + 1
                    upos = lambda blk, ip: t0 + blk * 12 + 2 * ip + 2
                    dpos = lambda blk, j2: t0 + blk * 12 + 8 + j2 + 1
                    spos = lambda blk, ip: s0 + blk * 4 + ip + 1
                    mpos = lambda blk, ip: v0 + blk * 8 + ip + 1
                    apos = lambda blk, j2: v0 + blk * 8 + 5 + j2

                    @block.tensor
                    def _(t, e=e, par=par, hf=hf, t0=t0, s0=s0, v0=v0,
                          pgf=list(pg_free_ss), puf=list(pu_free_vv),
                          pdf=list(pd_free_vv), ldxn=ld_xn):
                        C.wait(t, f"WQE{par}", wq_e)
                        if e == 0:
                            C.wait(t, "LDN", ldxn)
                            C.wait(t, "VV", vv_cnat)
                        gu3 = gub[par][:].rearrange("p (a n) -> p a n", a=8)
                        dn3 = dnb[par][:].rearrange("p (a n) -> p a n", a=4)
                        xn3 = xnTh[:].rearrange("p (a n) -> p a n", a=8)
                        a3 = act_sb[:].rearrange("p (a n) -> p a n", a=4)
                        for blk in range(4):
                            for ip in range(4):
                                idx = blk * 4 + ip
                                prev2 = idx - 2
                                if prev2 >= 0:
                                    C.wait(t, "SS", s0 + prev2 + 1)
                                    C.wait(t, "VV", v0 + (prev2 // 4) * 8 + (prev2 % 4) + 1)
                                else:
                                    if pgf[idx % 2] > 0:
                                        C.wait(t, "SS", pgf[idx % 2])
                                    if puf[idx % 2] > 0:
                                        C.wait(t, "VV", puf[idx % 2])
                                last = None
                                for k in range(8):
                                    last = t.matmul(out=pg[idx % 2][:], lhsT=gu3[:, k, ip * 128:(ip + 1) * 128],
                                                    rhs=xn3[:, k, blk * 512:(blk + 1) * 512],
                                                    start=(k == 0), stop=(k == 7))
                                C.inc(last, "TT")
                                last = None
                                for k in range(8):
                                    last = t.matmul(out=pu[idx % 2][:], lhsT=gu3[:, k, 512 + ip * 128:512 + (ip + 1) * 128],
                                                    rhs=xn3[:, k, blk * 512:(blk + 1) * 512],
                                                    start=(k == 0), stop=(k == 7))
                                C.inc(last, "TT")
                            for j2 in range(4):
                                C.wait(t, "VV", v0 + blk * 8 + 4)
                                dpar = j2 % 2
                                if j2 >= 2:
                                    C.wait(t, "VV", v0 + blk * 8 + 5 + (j2 - 2))
                                elif blk >= 1:
                                    C.wait(t, "VV", v0 + (blk - 1) * 8 + 5 + (j2 + 2))
                                elif pdf[dpar] > 0:
                                    C.wait(t, "VV", pdf[dpar])
                                last = None
                                for ip in range(4):
                                    lh = a3[:, ip, j2 * 128:(j2 + 1) * 128]
                                    t.matmul(out=pd[dpar][:, 0:512], lhsT=lh,
                                             rhs=dn3[:, ip, 0:512],
                                             start=(ip == 0), stop=(ip == 3))
                                    last = t.matmul(out=pd[dpar][:, 512:1024], lhsT=lh,
                                                    rhs=dn3[:, ip, 512:1024],
                                                    start=(ip == 0), stop=(ip == 3))
                                C.inc(last, "TT")
                    assert C.v["TT"] == t0 + 48, (C.v["TT"], t0)
                    wbuf_free_tt[par] = C.v["TT"]

                    @block.scalar
                    def _(s, t0=t0, v0=v0):
                        for blk in range(4):
                            for ip in range(4):
                                idx = blk * 4 + ip
                                C.wait(s, "TT", gpos(blk, ip))
                                pm = idx - 2
                                if pm >= 0:
                                    C.wait(s, "VV", v0 + (pm // 4) * 8 + (pm % 4) + 1)
                                i = s.activation(out=silu_sb[idx % 2][:], in_=pg[idx % 2][:],
                                                 func=AF.Silu)
                                C.inc(i, "SS")
                    assert C.v["SS"] == s0 + 16
                    pg_free_ss = [0, 0]
                    for idx in range(16):
                        pg_free_ss[idx % 2] = s0 + idx + 1

                    @block.vector
                    def _(v, e=e, hf=hf, t0=t0, s0=s0, aft=act_free_tt):
                        a3 = act_sb[:].rearrange("p (a n) -> p a n", a=4)
                        for blk in range(4):
                            for ip in range(4):
                                C.wait(v, "SS", spos(blk, ip))
                                C.wait(v, "TT", upos(blk, ip))
                                if blk >= 1:
                                    C.wait(v, "TT", t0 + (blk - 1) * 12 + 12)
                                elif aft > 0:
                                    C.wait(v, "TT", aft)
                                idx = blk * 4 + ip
                                i = v.tensor_tensor(out=a3[:, ip, :], in0=silu_sb[idx % 2][:],
                                                    in1=pu[idx % 2][:], op=ALU.mult)
                                C.inc(i, "VV")
                            for j2 in range(4):
                                C.wait(v, "TT", dpos(blk, j2))
                                jl = blk * 4 + j2
                                cc = hf * 16 + jl
                                accs = acc[:, jl * 1024:(jl + 1) * 1024]
                                if e < NE:
                                    i = v.scalar_tensor_tensor(
                                        out=accs, in0=pd[j2 % 2][:],
                                        scalar=c_nat[:, cc * NE + e:cc * NE + e + 1],
                                        in1=accs, op0=ALU.mult, op1=ALU.add)
                                else:
                                    i = v.tensor_tensor(out=accs, in0=pd[j2 % 2][:],
                                                        in1=accs, op=ALU.add)
                                C.inc(i, "VV")
                    assert C.v["VV"] == v0 + 32
                    pu_free_vv = [0, 0]
                    for idx in range(16):
                        pu_free_vv[idx % 2] = v0 + (idx // 4) * 8 + (idx % 4) + 1
                    act_free_tt = wbuf_free_tt[par]
                    pd_free_vv = [apos(3, 2), apos(3, 3)]

                half_t_end = C.v["TT"]
                vv_acc_done = C.v["VV"]

                for jj in range(16):
                    @block.sync
                    def _(sy, jj=jj, hf=hf, xf=xres_free_vv[jj % 2]):
                        if hf == 0 and jj == 0:
                            for nm in ("SDA", "SDAQ0", "SDAQ1", "LDH0", "LDH1", "SDB0", "SDB1", "SDV0", "SDV1", "SDD", "TT", "SS", "VV", "GG"):
                                C.wait(sy, nm, fenceB[nm])
                        if xf > 0:
                            C.wait(sy, "VV", xf)
                        i = sy.dma_start(out=xres[jj % 2][:],
                                         in_=x_d[(hf * 16 + jj) * 128:(hf * 16 + jj + 1) * 128, :])
                        C.inc(i, f"LXR{jj % 2}", 16)
                    ld_xr = C.v[f"LXR{jj % 2}"]

                    @block.vector
                    def _(v, jj=jj, ld_xr=ld_xr, obf=outb_free_od[jj % 2]):
                        C.wait(v, f"LXR{jj % 2}", ld_xr)
                        if jj == 0:
                            C.wait(v, "VV", vv_acc_done)
                        if obf > 0:
                            C.wait(v, f"OD{jj % 2}", obf)
                        i = v.tensor_tensor(out=outb[jj % 2][:],
                                            in0=acc[:, jj * 1024:(jj + 1) * 1024],
                                            in1=xres[jj % 2][:], op=ALU.add)
                        C.inc(i, "VV")
                    xres_free_vv[jj % 2] = C.v["VV"]
                    vv_out = C.v["VV"]

                    @block.sync
                    def _(sy, jj=jj, hf=hf, vv_out=vv_out):
                        C.wait(sy, "VV", vv_out)
                        i = sy.dma_start(out=out_d[(hf * 16 + jj) * 128:(hf * 16 + jj + 1) * 128, :],
                                         in_=outb[jj % 2][:])
                        C.inc(i, f"OD{jj % 2}", 16)
                    outb_free_od[jj % 2] = C.v[f"OD{jj % 2}"]
                out_done_od = (C.v["OD0"], C.v["OD1"])

            od0, od1 = C.v["OD0"], C.v["OD1"]
            sdd = C.v["SDD"]

            @block.sync
            def _(sy):
                C.wait(sy, "OD0", od0)
                C.wait(sy, "OD1", od1)
                if sdd > 0:
                    C.wait(sy, "SDD", sdd)

    return nc


def _host_constants():
    half = 32
    i32 = np.float32(32.0)
    freqs = np.power(np.float32(10000.0), -(np.arange(half, dtype=np.float32) / i32))
    ang = np.arange(S, dtype=np.float32)[:, None] * freqs[None, :]
    cos = np.cos(ang).astype(np.float32)
    sin = np.sin(ang).astype(np.float32)
    ropeA = np.empty((S, HD), np.float32)
    ropeB = np.empty((S, HD), np.float32)
    ropeA[:, 0::2] = cos
    ropeA[:, 1::2] = cos
    ropeB[:, 0::2] = -sin
    ropeB[:, 1::2] = sin
    dmask = np.zeros((4, 128, 512), np.float32)
    for j in range(4):
        kk = np.arange(128)[:, None] + j * 128
        qq = np.arange(512)[None, :]
        dmask[j] = (kk <= qq).astype(np.float32)
    ones = np.ones((128, 128), np.float32)
    ones1 = np.ones((1, 128), np.float32)
    ident = np.eye(128, dtype=np.float32)
    return ropeA, ropeB, dmask, ones, ones1, ident


def kernel(L0_attn_x, L0_attn_Wq, L0_attn_Wk, L0_attn_Wv, L0_attn_Wo, L0_attn_bo,
           L0_ffn_gate_weight, L0_ffn_e_score_correction_bias, L0_ffn_experts_gate_up,
           L0_ffn_experts_down, L0_ffn_shared_gate_w, L0_ffn_shared_up_w,
           L0_ffn_shared_down_w):
    x = np.ascontiguousarray(np.asarray(L0_attn_x, dtype=np.float32))
    wq = np.ascontiguousarray(np.asarray(L0_attn_Wq, dtype=np.float32))
    wk = np.ascontiguousarray(np.asarray(L0_attn_Wk, dtype=np.float32))
    wv = np.ascontiguousarray(np.asarray(L0_attn_Wv, dtype=np.float32))
    wo = np.ascontiguousarray(np.asarray(L0_attn_Wo, dtype=np.float32))
    bo = np.asarray(L0_attn_bo, dtype=np.float32).reshape(1, D)
    gwT = np.ascontiguousarray(np.asarray(L0_ffn_gate_weight, dtype=np.float32).T)
    eb = np.asarray(L0_ffn_e_score_correction_bias, dtype=np.float32).reshape(1, NE)
    guT = np.ascontiguousarray(np.transpose(np.asarray(L0_ffn_experts_gate_up, dtype=np.float32), (0, 2, 1)))
    dnT = np.ascontiguousarray(np.transpose(np.asarray(L0_ffn_experts_down, dtype=np.float32), (0, 2, 1)))
    sgwT = np.ascontiguousarray(np.asarray(L0_ffn_shared_gate_w, dtype=np.float32).T)
    suwT = np.ascontiguousarray(np.asarray(L0_ffn_shared_up_w, dtype=np.float32).T)
    sdwT = np.ascontiguousarray(np.asarray(L0_ffn_shared_down_w, dtype=np.float32).T)
    ropeA, ropeB, dmask, ones, ones1, ident = _host_constants()

    if "nc" not in _CACHE:
        _CACHE["nc"] = build_program()
    nc = _CACHE["nc"]

    shared = dict(wq=wq, wk=wk, wv=wv, wo=wo, bo=bo, gwT=gwT, eb=eb, guT=guT,
                  dnT=dnT, sgwT=sgwT, suwT=suwT, sdwT=sdwT, ropeA=ropeA,
                  ropeB=ropeB, dmask=dmask, ones=ones, ones1=ones1, ident=ident)
    in_maps = []
    for b in range(NCORE):
        m = dict(shared)
        m["x"] = np.ascontiguousarray(x[b])
        m["xT"] = np.ascontiguousarray(x[b].T)
        in_maps.append(m)

    res = run_bass_kernel_spmd(nc, in_maps, core_ids=list(range(NCORE)))
    out = np.stack([np.asarray(res.results[b]["out"]) for b in range(NCORE)], axis=0)
    return out
